# version 12
# baseline (speedup 1.0000x reference)
import numpy as np
import ml_dtypes
import concourse.bass as bass
import concourse.mybir as mybir
from concourse.bass_utils import run_bass_kernel_spmd

F32 = mybir.dt.float32
BF16 = mybir.dt.bfloat16
ALU = mybir.AluOpType
ACT = mybir.ActivationFunctionType
AX = mybir.AxisListType

D = 1024
NCH = 8
NCORE = 8
LN_EPS = 1e-5
RMS_EPS = 1e-6
EXP_CLAMP = 16.2
RECIP_MAX = 1e7

ENGINES = ("sync", "tensor", "vector", "scalar", "gpsimd")
import os as _os
WIDE_GROUPS = set(_os.environ.get("WIDE_GROUPS", "twist,conv,misc").split(","))


def _w(group):
    return group in WIDE_GROUPS
ET, EV, ES, EG, EY = "tensor", "vector", "scalar", "gpsimd", "sync"


class Plan:
    def __init__(self, nc):
        self.nc = nc
        self.q = {e: [] for e in ENGINES}
        self.sem_handles = {}
        self.cnt = {}
        self.lastwait = {}

    def sem(self, name):
        if name not in self.sem_handles:
            self.sem_handles[name] = self.nc.semaphore(name).__enter__()
            self.cnt[name] = 0
        return self.sem_handles[name]

    def emit(self, eng, fn, inc=None, inck=1, wide=False):
        if eng == "vector":
            if getattr(self, "_last_v_narrow", False):
                self.wait(eng, getattr(self, "_last_v", None))
            if inc is None and not wide:
                inc = "dvo"
        h = self.sem(inc) if inc else None
        if inc:
            self.cnt[inc] += inck
            mark = (inc, self.cnt[inc])
        else:
            mark = None

        def closure(e, fn=fn, h=h, inck=inck, do=inc is not None):
            ins = fn(e)
            if do:
                assert ins is not None
                ins.then_inc(h, inck)
        self.q[eng].append(closure)
        if eng == "vector":
            if mark is not None:
                self._last_v = mark
            self._last_v_narrow = not wide
        return mark

    def wait(self, eng, mark):
        if mark is None:
            return
        sem, val = mark
        if val <= 0:
            return
        h = self.sem(sem)
        key = (eng, sem)
        if self.lastwait.get(key, 0) >= val:
            return
        self.lastwait[key] = val

        def closure(e, h=h, val=val):
            e.wait_ge(h, val)
        self.q[eng].append(closure)

    def waits(self, eng, marks):
        for m in marks:
            self.wait(eng, m)

    def mark(self, sem):
        self.sem(sem)
        return (sem, self.cnt[sem])


def build_program(TC=4096, TB=1024, state_tokens=2048, seq_total=16384, stop_after=None):
    NBLK = TC // TB
    TPB = TB // 128
    NSL = TB // 512
    NST = TB // 256
    SBLK = max(1, min(NBLK, state_tokens // TB))
    HCOL = NBLK * TPB

    nc = bass.Bass("TRN2", target_bir_lowering=False, debug=False,
                   detect_race_conditions=False)

    dr = {}

    def din(name, shape, dt=F32):
        dr[name] = nc.dram_tensor(name, shape, dt, kind="ExternalInput").ap()
        return dr[name]

    x_d = din("x", [TC, D])
    xh_d = din("xhalo", [6, D])
    din("bm", [2]); din("smask", [1])
    din("state_scale", [1]); din("mem_scale", [1]); din("ar", [1])
    din("conv_w", [D, 7]); din("norm1_w", [D]); din("norm2_w", [D])
    din("bdecay", [D]); din("bgate", [D]); din("bc", [D])
    for i, (a, bb) in enumerate([(D, D), (D, 512), (512, 256), (256, 1)]):
        din(f"cn_w{i+1}", [a, bb]); din(f"cn_b{i+1}", [bb])
    for i, (a, bb) in enumerate([(D, 512), (512, 256), (256, 1)]):
        din(f"gc_w{i+1}", [a, bb]); din(f"gc_b{i+1}", [bb])
    din("Wk", [D, 256]); din("Mt", [256, 256]); din("Wmem", [256, D])
    din("Wdecay_t", [NCH, NCH, 128, 128])
    din("Wgate_t", [NCH, NCH, 128, 128])
    din("Wc_t", [NCH, 3 * NCH, 128, 128])
    din("Wg_t", [32, D, 128])
    din("Wup_t", [32, D, 128])
    din("Wdown", [4096, D])

    y_d = nc.dram_tensor("y", [TC, D], F32, kind="ExternalOutput").ap()
    cc_in = [nc.dram_tensor(f"cc_in{i}", [128, NCH, 2], F32) for i in range(2)]
    cc_out = [nc.dram_tensor(f"cc_out{i}", [128, NCH, 2], F32) for i in range(2)]

    from contextlib import ExitStack
    es = ExitStack()

    def sbt(name, shape, dt=F32):
        return es.enter_context(nc.sbuf_tensor(name, shape, dt))

    ident = sbt("ident", [128, 128])
    ones_row = sbt("ones_row", [1, 128])
    cnw1 = sbt("cnw1", [128, NCH, D], BF16)
    cnw2 = sbt("cnw2", [128, NCH, 512], BF16)
    cnw3 = sbt("cnw3", [128, 4, 256], BF16)
    cnw4 = sbt("cnw4", [128, 2], BF16)
    gcw1 = sbt("gcw1", [128, NCH, 512], BF16)
    gcw2 = sbt("gcw2", [128, 4, 256], BF16)
    gcw3 = sbt("gcw3", [128, 2], BF16)
    cnb1 = sbt("cnb1", [128, NCH]); cnb2 = sbt("cnb2", [128, 4])
    cnb3 = sbt("cnb3", [128, 2]); cnb4 = sbt("cnb4", [1, 1])
    gcb1 = sbt("gcb1", [128, 4]); gcb2 = sbt("gcb2", [128, 2]); gcb3 = sbt("gcb3", [1, 1])
    bdec = sbt("bdec", [128, NCH]); bgat = sbt("bgat", [128, NCH])
    bc_c = sbt("bc_c", [128, NCH])
    n1w = sbt("n1w", [128, NCH]); n2w = sbt("n2w", [128, NCH])
    convw = sbt("convw", [128, NCH, 7])
    sc_raw = sbt("sc_raw", [1, 8])
    ssm_col = sbt("ssm_col", [128, 1]); ms_col = sbt("ms_col", [128, 1])
    bm_cols = sbt("bm_cols", [128, 2])
    a2_col = [sbt(f"a2_col{i}", [128, 1]) for i in range(2)]
    c2_col = [sbt(f"c2_col{i}", [128, 1]) for i in range(2)]

    NT1 = NBLK * TPB + 1
    mu1 = sbt("mu1", [128, NT1]); iv1 = sbt("iv1", [128, NT1]); nm1 = sbt("nm1", [128, NT1])
    mu2 = sbt("mu2", [128, NT1]); iv2 = sbt("iv2", [128, NT1]); nm2 = sbt("nm2", [128, NT1])
    rr2 = sbt("rr2", [128, NT1]); rr3 = sbt("rr3", [128, NT1])
    vscr = sbt("vscr", [128, NT1]); cscr = sbt("cscr", [128, NT1])
    bn6 = sbt("bn6", [128, 12]); bn2s = sbt("bn2s", [128, 2])
    z1c = sbt("z1c", [128, NT1]); z1pc = sbt("z1pc", [128, NT1])
    ztc = sbt("ztc", [128, NT1]); zsc = sbt("zsc", [128, NT1])
    cpl = sbt("cpl", [128, NT1]); ncpl = sbt("ncpl", [128, NT1])
    accb = sbt("accb", [128, NCH, 2])
    sums = [sbt(f"sums{i}", [128, NCH]) for i in range(2)]
    arstg = sbt("arstg", [128, NCH, 2])
    mcols = sbt("mcols", [128, NCH]); mcols_b = sbt("mcols_b", [128, NCH], BF16)
    gnet = sbt("gnet", [128, 6], BF16)
    gsc = sbt("gsc", [1, 8])
    halo6 = sbt("halo6", [128, NCH, 6], BF16)
    lcd_carry = sbt("lcd_carry", [128, NCH])
    cs2_carry = sbt("cs2_carry", [128, NCH])
    zrow = sbt("zrow", [1, 512])
    bstg = sbt("bstg", [1, 1024])
    twA = sbt("twA", [128, 2, 256]); twB = sbt("twB", [128, 2, 256])
    sqs = sbt("sqs", [128, D])

    xbuf = sbt("xbuf", [128, TPB + 1, D])
    aux = sbt("aux", [128, TPB + 1, D])
    h2T = sbt("h2T", [128, NCH, TB], BF16)

    HT_W = NCH * (TB + 6) // 2
    NRING = 8
    SLOTW = 1536
    SW_NEED = NRING * SLOTW + 1024 + 256
    SCRW = max(9216 + 1024 + HT_W, SW_NEED)
    scr = sbt("scr", [128, SCRW])

    def _reshape(ap, shape):
        if len(shape) == 1:
            return ap
        names = [f"d{i}" for i in range(len(shape))]
        spec = "p (" + " ".join(names) + ") -> p " + " ".join(names)
        kw = {n: s for n, s in zip(names[1:], shape[1:])}
        return ap.rearrange(spec, **kw)

    def scr_f32(off, shape):
        n = int(np.prod(shape))
        assert off + n <= SCRW, (off, shape)
        return _reshape(scr[:, off:off + n], shape)

    def scr_bf16(off, shape):
        n = int(np.prod(shape))
        assert n % 2 == 0 and off + n // 2 <= SCRW, (off, shape)
        return _reshape(scr[:, off:off + n // 2].bitcast(BF16), shape)

    HT_OFF = SCRW - HT_W
    hT = scr_bf16(HT_OFF, [NCH, TB + 6])
    outT = scr_bf16(0, [NCH, 512])
    g1b = scr_bf16(2048, [NCH, 512])
    g2b = scr_bf16(4096, [4, 512])
    g3b = scr_bf16(5120, [2, 512])
    wdg_h = scr_bf16(0, [2, 4, NCH, 128])
    st_scr = [scr_f32(4096 + i * 512, [512]) for i in range(4)]
    memT_b = scr_bf16(0, [NCH, TB])
    kT_s = scr_bf16(9728, [2, 512])
    moT_s = scr_bf16(6144, [2, 512])
    wk_sl = scr_bf16(7424, [NCH, 256])
    mt_sl = scr_bf16(8448, [2, 256])
    wme_sl = scr_bf16(8704, [2, D])
    cv_scr = [scr_f32(4096 + i * 512, [512]) for i in range(2)]
    wc_ring = [scr_bf16(5120 + i * 1536, [3 * NCH, 128]) for i in range(2)]
    combT = [scr_f32(8192 + i * 512, [512]) for i in range(2)]
    assert 10240 <= HT_OFF
    swg_wg = [scr_bf16(i * SLOTW, [NCH, 128]) for i in range(NRING)]
    swg_wu = [scr_bf16(i * SLOTW + 512, [NCH, 128]) for i in range(NRING)]
    swg_wd = [scr_bf16(i * SLOTW + 1024, [NCH, 128]) for i in range(NRING)]
    SW_OFF = NRING * SLOTW
    prod_buf = scr_bf16(SW_OFF, [NCH, 256])
    silu_sb = [scr_bf16(SW_OFF + 1024 + i * 128, [256]) for i in range(2)]

    P = [es.enter_context(nc.psum_tensor(f"P{i}", [128, 512], F32)) for i in range(4)]
    PF = es.enter_context(nc.psum_tensor("PF", [128, 2048], F32))

    pl = Plan(nc)
    for s in ("pe", "dv", "ac", "gp", "xld", "xst", "winit", "binit", "ccs", "ccc", "mxs", "yac"):
        pl.sem(s)
    swg_sem = [f"sg{i}" for i in range(NRING)]
    wc_sem = ["wc0", "wc1"]
    for s in swg_sem + wc_sem:
        pl.sem(s)

    p_free = [None, None, None, None]
    pf_free = None

    def mm(bank_ap, lhsT_ap, rhs_ap, start, stop, inc=True):
        def f(t, o=bank_ap, a=lhsT_ap, r=rhs_ap, st=start, sp=stop):
            return t.matmul(o, a, r, start=st, stop=sp)
        return pl.emit(ET, f, inc="pe" if inc else None)

    def transp(bank_ap, in_ap, inc=True, np_=128, start=True, stop=True):
        def f(t, o=bank_ap, i=in_ap, n=np_, st=start, sp=stop):
            return t.matmul(o, i, ident[0:n, 0:n], start=st, stop=sp,
                            is_transpose=True)
        return pl.emit(ET, f, inc="pe" if inc else None)

    def stats_tile(src, col, mu_t, npart=128):
        pl.emit(ES, lambda s, x=src, c=col, n=npart: s.activation(
            sqs[0:n, :], x[0:n, :], ACT.Square,
            accum_out=vscr[0:n, c:c + 1]), inc="ac")
        return pl.emit(EV, lambda v, x=src, c=col, n=npart: v.reduce_sum(
            mu_t[0:n, c:c + 1], x[0:n, :], axis=AX.X), inc="dv")

    def stat_cols(mu_t, iv_t, nm_t, c0, c1, eps, r_t=None, r_only=False):
        V = vscr[:, c0:c1]; MU = mu_t[:, c0:c1]
        T0 = cscr[:, c0:c1]; A = zsc[:, c0:c1]
        pl.wait(EV, pl.mark("ac"))

        def rsqrt_into(dst, scratch):
            m = pl.mark("dv")
            pl.wait(ES, m)
            pl.emit(ES, lambda s: s.activation(dst, T0, ACT.Sqrt), inc="ac")
            pl.wait(EV, pl.mark("ac"))
            pl.emit(EV, lambda v: v.reciprocal(dst, dst))
            pl.emit(EV, lambda v: v.tensor_mul(scratch, dst, dst))
            pl.emit(EV, lambda v: v.tensor_mul(scratch, scratch, T0))
            pl.emit(EV, lambda v: v.tensor_scalar(
                scratch, scratch, -0.5, 1.5, ALU.mult, ALU.add))
            pl.emit(EV, lambda v: v.tensor_mul(dst, dst, scratch))

        if r_only:
            R = r_t[:, c0:c1]
            pl.emit(EV, lambda v: v.tensor_scalar(
                T0, V, float(1.0 / D), float(RMS_EPS), ALU.mult, ALU.add),
                inc="dv")
            rsqrt_into(R, A)
            return pl.emit(EV, lambda v: v.engine_nop(), inc="dv")

        IV = iv_t[:, c0:c1]; NM = nm_t[:, c0:c1]
        pl.emit(EV, lambda v: v.tensor_scalar_mul(MU, MU, float(1.0 / D)))
        pl.emit(EV, lambda v: v.tensor_scalar_mul(A, V, float(1.0 / D)))
        pl.emit(EV, lambda v: v.tensor_mul(V, MU, MU))
        pl.emit(EV, lambda v: v.tensor_sub(A, A, V))
        pl.emit(EV, lambda v: v.tensor_scalar_add(T0, A, float(eps)), inc="dv")
        rsqrt_into(IV, V)
        pl.emit(EV, lambda v: v.scalar_tensor_tensor(NM, MU, -1.0, IV,
                                                     ALU.mult, ALU.mult))
        mret = pl.emit(EV, lambda v: v.engine_nop(), inc="dv")
        if r_t is not None:
            R = r_t[:, c0:c1]
            pl.emit(EV, lambda v: v.tensor_mul(V, IV, IV))
            pl.emit(EV, lambda v: v.tensor_mul(V, V, A))
            pl.emit(EV, lambda v: v.tensor_scalar_add(T0, V, float(RMS_EPS)),
                    inc="dv")
            rsqrt_into(R, V)
            mret = pl.emit(EV, lambda v: v.engine_nop(), inc="dv")
        return mret

    pl.emit(EG, lambda g: g.memset(ident[:, :], 0.0), inc="gp")
    pl.wait(EG, pl.mark("gp"))
    pl.emit(EG, lambda g: g.affine_select(
        out=ident[:, :], in_=ident[:, :], compare_op=ALU.not_equal,
        fill=1.0, base=0, pattern=[[-1, 128]], channel_multiplier=1), inc="gp")
    pl.emit(EG, lambda g: g.memset(ones_row[:, :], 1.0), inc="gp")
    for bname in (sums[0], sums[1], lcd_carry, cs2_carry, mu1, mu2, vscr,
                  cscr, zsc, z1c, z1pc, ztc, cpl, ncpl, iv1, nm1, iv2, nm2,
                  rr2, rr3, zrow, gsc, bstg):
        pl.emit(EG, lambda g, t=bname: g.memset(t[:, :], 0.0), inc="gp")
    pl.emit(EG, lambda g: g.memset(halo6[:, :, :], 0.0), inc="gp")
    setup_done = pl.mark("gp")

    def ldw(dst, src):
        pl.emit(EG, lambda g, d=dst, s=src: g.dma_start(out=d, in_=s),
                inc="winit", inck=16)

    ldw(cnw1[:, :, :], dr["cn_w1"].rearrange("(c p) m -> p c m", p=128))
    ldw(cnw2[:, :, :], dr["cn_w2"].rearrange("(c p) m -> p c m", p=128))
    ldw(cnw3[:, :, :], dr["cn_w3"].rearrange("(c p) m -> p c m", p=128))
    ldw(gcw1[:, :, :], dr["gc_w1"].rearrange("(c p) m -> p c m", p=128))
    ldw(gcw2[:, :, :], dr["gc_w2"].rearrange("(c p) m -> p c m", p=128))
    ldw(cnb4[:, :], dr["cn_b4"][None, :])
    ldw(gcb3[:, :], dr["gc_b3"][None, :])
    ldw(convw[:, :, :], dr["conv_w"].rearrange("(c p) k -> p c k", p=128))
    ldw(sc_raw[0:1, 0:1], dr["state_scale"][None, :])
    ldw(sc_raw[0:1, 1:2], dr["mem_scale"][None, :])
    ldw(sc_raw[0:1, 2:3], dr["ar"][None, :])
    ldw(sc_raw[0:1, 3:4], dr["smask"][None, :])
    ldw(sc_raw[0:1, 4:6], dr["bm"][None, :])
    winit_done = pl.mark("winit")

    col_specs = [
        ("cn_b1", cnb1, NCH), ("cn_b2", cnb2, 4), ("cn_b3", cnb3, 2),
        ("gc_b1", gcb1, 4), ("gc_b2", gcb2, 2),
        ("bdecay", bdec, NCH), ("bgate", bgat, NCH), ("bc", bc_c, NCH),
        ("norm1_w", n1w, NCH), ("norm2_w", n2w, NCH),
        ("cn_w4", cnw4, 2), ("gc_w3", gcw3, 2),
    ]
    pl.waits(ET, [setup_done])
    nvec = 0
    for (name, dst, ncol) in col_specs:
        nvec += 1
        pl.emit(EG, lambda g, name=name, n=ncol: g.dma_start(
            out=bstg[0:1, 0:n * 128], in_=dr[name].rearrange("(a) -> 1 a")
            if False else dr[name][None, :]), inc="binit", inck=16)
        m_row = ("binit", nvec * 16)
        bank = 3
        pl.waits(ET, [m_row, p_free[bank]])
        for c in range(ncol):
            transp(P[bank][:, c:c + 1], bstg[0:1, c * 128:(c + 1) * 128],
                   inc=(c == ncol - 1), np_=1,
                   start=(c == 0), stop=(c == ncol - 1))
        pl.wait(ES, pl.mark("pe"))
        pl.emit(ES, lambda s, d=dst, n=ncol, b=bank: s.copy(
            d[:, 0:n], P[b][:, 0:n]), inc="ac")
        m_bst = pl.mark("ac")
        p_free[bank] = m_bst
        pl.wait(EG, m_bst)
    bcols_done = pl.mark("ac")

    pl.wait(EV, winit_done)
    pl.wait(EV, bcols_done)
    m_ssm = pl.emit(EV, lambda v: v.tensor_mul(
        sc_raw[0:1, 6:7], sc_raw[0:1, 0:1], sc_raw[0:1, 3:4]), inc="dv")
    pl.waits(ET, [setup_done, m_ssm])
    mm(P[0][:, 0:1], ones_row[:, :], sc_raw[0:1, 6:7], True, False, inc=False)
    mm(P[0][:, 1:2], ones_row[:, :], sc_raw[0:1, 1:2], False, False, inc=False)
    mm(P[0][:, 2:3], ones_row[:, :], sc_raw[0:1, 4:5], False, False, inc=False)
    mm(P[0][:, 3:4], ones_row[:, :], sc_raw[0:1, 5:6], False, True, inc=True)
    pl.wait(ES, pl.mark("pe"))
    pl.emit(ES, lambda s: s.copy(ssm_col[:, :], P[0][:, 0:1]), inc="ac")
    pl.emit(ES, lambda s: s.copy(ms_col[:, :], P[0][:, 1:2]), inc="ac")
    pl.emit(ES, lambda s: s.copy(bm_cols[:, :], P[0][:, 2:4]), inc="ac")
    p_free[0] = pl.mark("ac")

    zrow_free = None

    def chain_slice(src_fn, width=512):
        nonlocal zrow_free
        specs = [(NCH, NCH, cnw1, cnb1, None, g1b),
                 (4, NCH, cnw2, cnb2, g1b, g2b),
                 (2, 4, cnw3, cnb3, g2b, g3b)]
        pl.wait(ET, pl.mark("ac"))
        for (no, nk, wt, bt, src_g, dst_g) in specs:
            if src_g is not None:
                pl.wait(ET, pl.mark("ac"))
            for o in range(no):
                bank = o % 2
                pl.wait(ET, p_free[bank])
                for k in range(nk):
                    rhs = src_fn(k) if src_g is None else src_g[:, k, 0:width]
                    mm(P[bank][:, 0:width], wt[:, k, o * 128:(o + 1) * 128],
                       rhs, k == 0, k == nk - 1, inc=(k == nk - 1))
                pl.wait(ES, pl.mark("pe"))
                pl.emit(ES, lambda s, o=o, b=bank, w=width, dg=dst_g, bt=bt:
                        s.activation(dg[:, o, 0:w], P[b][:, 0:w], ACT.Gelu,
                                     bias=bt[:, o:o + 1]), inc="ac")
                p_free[bank] = pl.mark("ac")
        bank = 2
        pl.waits(ET, [p_free[bank], pl.mark("ac")])
        for k in range(2):
            mm(P[bank][0:1, 0:width], cnw4[:, k:k + 1], g3b[:, k, 0:width],
               k == 0, k == 1, inc=(k == 1))
        pl.waits(ES, [pl.mark("pe"), zrow_free])
        mz = pl.emit(ES, lambda s, w=width: s.activation(
            zrow[0:1, 0:w], P[2][0:1, 0:w], ACT.Identity, bias=cnb4[0:1, 0:1]),
            inc="ac")
        p_free[bank] = pl.mark("ac")
        return mz

    def zrow_to_cols(dst, col0, nq, m_z, last_np=128):
        nonlocal zrow_free
        pl.wait(ET, m_z)
        bank = 3
        pl.wait(ET, p_free[bank])
        for q in range(nq):
            npq = last_np if q == nq - 1 else 128
            transp(P[bank][0:npq, q * 128:q * 128 + 1],
                   zrow[0:1, q * 128:q * 128 + npq], inc=(q == nq - 1), np_=1,
                   start=(q == 0), stop=(q == nq - 1))
        zrow_free = pl.mark("pe")
        pl.wait(ES, zrow_free)
        def f(s, d=dst, c=col0, n=nq, b=bank):
            if n > 1:
                src = P[b][:, 0:n * 128].rearrange("p (q o) -> p q o", o=128)[:, :, 0]
            else:
                src = P[b][:, 0:1]
            return s.copy(d[:, c:c + n], src)
        mc = pl.emit(ES, f, inc="ac")
        p_free[bank] = pl.mark("ac")
        return mc

    twist_prev = [None]

    def coupling_from_z(zsrc, c0, c1, idx, m_ready):
        pl.waits(ES, [m_ready, twist_prev[0]])
        pl.emit(ES, lambda s: s.activation(
            cpl[:, c0:c1], zsrc[:, c0:c1], ACT.Tanh, scale=0.5), inc="ac")
        m = pl.mark("ac")
        pl.wait(EV, m)
        pl.emit(EV, lambda v: v.tensor_scalar(
            cpl[:, c0:c1], cpl[:, c0:c1], c2_col[idx][:, 0:1],
            a2_col[idx][:, 0:1], ALU.mult, ALU.add))
        pl.emit(EV, lambda v: v.tensor_scalar_mul(ncpl[:, c0:c1], cpl[:, c0:c1], -1.0))
        return pl.emit(EV, lambda v: v.engine_nop(), inc="dv")

    twist_slot_free = [None, None]

    def twist_tile(buf, tile, npart, ccol, m_cpl, slot=0):
        t = tile
        A = buf[0:npart, t, 0:256]; Bq = buf[0:npart, t, 256:512]
        C = buf[0:npart, t, 512:768]; Dq = buf[0:npart, t, 768:1024]
        cpc = cpl[0:npart, ccol:ccol + 1]; ncc = ncpl[0:npart, ccol:ccol + 1]
        tA = twA[0:npart, slot, :]; tB = twB[0:npart, slot, :]
        pl.waits(EV, [m_cpl, twist_slot_free[slot]])
        pl.emit(EV, lambda v: v.tensor_copy(tA, A), wide=_w('twist'))
        pl.emit(EV, lambda v: v.tensor_copy(tB, Bq), wide=_w('twist'))
        pl.emit(EV, lambda v: v.scalar_tensor_tensor(A, C, cpc, A, ALU.mult, ALU.add), wide=_w('twist'))
        pl.emit(EV, lambda v: v.scalar_tensor_tensor(C, tA, ncc, C, ALU.mult, ALU.add), wide=_w('twist'))
        pl.emit(EV, lambda v: v.scalar_tensor_tensor(Bq, Dq, ncc, Bq, ALU.mult, ALU.add), wide=_w('twist'))
        m4 = pl.emit(EV, lambda v: v.scalar_tensor_tensor(Dq, tB, cpc, Dq, ALU.mult, ALU.add), inc="dvo", wide=_w('twist'))
        pl.wait(EV, m4)
        mfin = pl.emit(EV, lambda v: v.engine_nop(), inc="dv")
        twist_prev[0] = mfin
        twist_slot_free[slot] = mfin
        return mfin

    xbuf_free = None
    aux_free = None
    ht_free = None
    m_sums1 = None
    accb_free = None

    for b in range(NBLK):
        halo = (b == 0)
        col0 = b * TPB
        pl.wait(EY, xbuf_free)
        pl.emit(EY, lambda y, b=b: y.dma_start(
            out=xbuf[:, 0:TPB, :],
            in_=x_d[b * TB:(b + 1) * TB, :].rearrange("(k p) d -> p k d", p=128)),
            inc="xld", inck=16)
        if halo:
            pl.emit(EY, lambda y: y.dma_start(
                out=xbuf[0:6, TPB, :], in_=xh_d), inc="xld", inck=16)
        m_xld = pl.mark("xld")

        pl.wait(EV, m_xld)
        pl.wait(ES, m_xld)
        for s_ in range(NSL):
            tls = [(s_ * 4 + q, col0 + s_ * 4 + q, 128) for q in range(4)]
            if halo and s_ == NSL - 1:
                tls.append((TPB, HCOL, 6))
            for (t, col, npart) in tls:
                stats_tile(xbuf[:, t, :], col, mu1, npart)
            if halo and s_ == NSL - 1:
                stat_cols(mu1, iv1, nm1, col0 + s_ * 4, col0 + s_ * 4 + 4,
                          LN_EPS)
                m_stats = stat_cols(mu1, iv1, nm1, HCOL, HCOL + 1, LN_EPS)
            else:
                m_stats = stat_cols(mu1, iv1, nm1, col0 + s_ * 4,
                                    col0 + s_ * 4 + 4, LN_EPS)
            pl.waits(ES, [m_stats, aux_free])
            for (t, col, npart) in tls:
                pl.emit(ES, lambda s, t=t, c=col, n=npart: s.activation(
                    aux[0:n, t, :], xbuf[0:n, t, :], ACT.Identity,
                    scale=iv1[0:n, c:c + 1], bias=nm1[0:n, c:c + 1]), inc="ac")
            m_lnx = pl.mark("ac")
            pl.wait(ET, m_lnx)
            for c in range(NCH):
                bank = (s_ * NCH + c) % 2
                pl.waits(ET, [p_free[bank], ht_free])
                for q in range(4):
                    transp(P[bank][:, q * 128:(q + 1) * 128],
                           aux[:, s_ * 4 + q, c * 128:(c + 1) * 128], inc=(q == 3),
                           start=(q == 0), stop=(q == 3))
                pl.waits(ES, [pl.mark("pe"), accb_free])
                pl.emit(ES, lambda s, c=c, s_=s_, b2=bank: s.activation(
                    hT[:, c, 6 + s_ * 512: 6 + (s_ + 1) * 512], P[b2][:, 0:512],
                    ACT.Identity, accum_out=accb[:, c, s_:s_ + 1]), inc="ac")
                p_free[bank] = pl.mark("ac")
        xbuf_free = m_lnx
        m_evac = pl.mark("ac")

        pl.wait(EV, m_evac)
        pl.emit(EV, lambda v: v.tensor_reduce(
            accb[:, :, 0:1], accb[:, :, :], AX.X, ALU.add))
        m_sums1 = pl.emit(EV, lambda v: v.tensor_add(
            sums[0][:, :], sums[0][:, :], accb[:, :, 0]), inc="dv")
        accb_free = m_sums1

        for s_ in range(NSL):
            mz = chain_slice(lambda k, s_=s_: hT[:, k, 6 + s_ * 512:6 + (s_ + 1) * 512])
            zrow_to_cols(z1c, col0 + s_ * 4, 4, mz)
        ht_free = pl.mark("pe")
        aux_free = ht_free

    pl.waits(ET, [ht_free])
    bank = 2
    pl.wait(ET, p_free[bank])
    for c in range(NCH):
        transp(P[bank][:, c * 64:c * 64 + 6],
               aux[0:6, TPB, c * 128:(c + 1) * 128], inc=(c == NCH - 1), np_=6,
               start=(c == 0), stop=(c == NCH - 1))
    pl.wait(ES, pl.mark("pe"))
    def f_he(s, b2=bank):
        src = P[b2][:, 0:NCH * 64].rearrange("p (c o) -> p c o", o=64)
        return s.copy(hT[:, :, 0:6], src[:, :, 0:6])
    pl.emit(ES, f_he, inc="ac")
    p_free[bank] = pl.mark("ac")
    mzh = chain_slice(lambda k: hT[:, k, 0:6], width=6)
    zrow_to_cols(z1c, HCOL, 1, mzh, last_np=6)
    ht_free = pl.mark("pe")
    aux_free = ht_free

    def allreduce_gf(idx, m_ready):
        ci, co = cc_in[idx], cc_out[idx]
        pl.wait(EV, m_ready)
        pl.emit(EV, lambda v: v.tensor_scalar_mul(
            arstg[:, :, 0], sums[idx][:, :], bm_cols[:, 0:1]))
        m_stg = pl.emit(EV, lambda v: v.tensor_scalar_mul(
            arstg[:, :, 1], sums[idx][:, :], bm_cols[:, 1:2]), inc="dv")
        pl.wait(EG, m_stg)
        pl.emit(EG, lambda g: g.dma_start(
            out=ci.ap(), in_=arstg[:, :, :]), inc="ccs", inck=16)
        pl.wait(EG, pl.mark("ccs"))
        pl.emit(EG, lambda g: g.collective_compute(
            "AllReduce", ALU.add, replica_groups=[list(range(NCORE))],
            ins=[ci.ap().opt()], outs=[co.ap().opt()]), inc="ccc", inck=1)
        pl.wait(EG, pl.mark("ccc"))
        pl.emit(EG, lambda g: g.dma_start(
            out=arstg[:, :, :], in_=co.ap()), inc="ccs", inck=16)
        m_ld = pl.mark("ccs")
        pl.wait(EV, m_ld)
        pl.emit(EV, lambda v: v.tensor_scalar_mul(
            arstg[:, :, 0], arstg[:, :, 0], bm_cols[:, 0:1]))
        pl.emit(EV, lambda v: v.tensor_scalar_mul(
            arstg[:, :, 1], arstg[:, :, 1], bm_cols[:, 1:2]))
        pl.emit(EV, lambda v: v.tensor_add(mcols[:, :], arstg[:, :, 0], arstg[:, :, 1]))
        pl.emit(EV, lambda v: v.tensor_scalar_mul(
            mcols[:, :], mcols[:, :], float(1.0 / seq_total)))
        m_m = pl.emit(EV, lambda v: v.tensor_copy(mcols_b[:, :], mcols[:, :]),
                      inc="dv")
        pl.wait(ET, m_m)
        bank = 0
        pl.wait(ET, p_free[bank])
        for o in range(4):
            for k in range(NCH):
                mm(P[bank][:, o:o + 1], gcw1[:, k, o * 128:(o + 1) * 128],
                   mcols_b[:, k:k + 1], (o == 0 and k == 0),
                   (o == 3 and k == NCH - 1), inc=(o == 3 and k == NCH - 1))
        pl.wait(ES, pl.mark("pe"))
        for o in range(4):
            pl.emit(ES, lambda s, o=o, b=bank: s.activation(
                gnet[:, o:o + 1], P[b][:, o:o + 1], ACT.Gelu,
                bias=gcb1[:, o:o + 1]), inc="ac")
        m_g1 = pl.mark("ac")
        p_free[bank] = m_g1
        bank = 1
        pl.waits(ET, [m_g1, p_free[bank]])
        for o in range(2):
            for k in range(4):
                mm(P[bank][:, o:o + 1], gcw2[:, k, o * 128:(o + 1) * 128],
                   gnet[:, k:k + 1], (o == 0 and k == 0),
                   (o == 1 and k == 3), inc=(o == 1 and k == 3))
        pl.wait(ES, pl.mark("pe"))
        for o in range(2):
            pl.emit(ES, lambda s, o=o, b=bank: s.activation(
                gnet[:, 4 + o:5 + o], P[b][:, o:o + 1], ACT.Gelu,
                bias=gcb2[:, o:o + 1]), inc="ac")
        m_g2 = pl.mark("ac")
        p_free[bank] = m_g2
        bank = 2
        pl.waits(ET, [m_g2, p_free[bank]])
        for k in range(2):
            mm(P[bank][0:1, 0:1], gcw3[:, k:k + 1], gnet[:, 4 + k:5 + k],
               k == 0, k == 1, inc=(k == 1))
        pl.wait(ES, pl.mark("pe"))
        pl.emit(ES, lambda s, b=bank: s.copy(gsc[0:1, 0:1], P[b][0:1, 0:1]), inc="ac")
        m_z3 = pl.mark("ac")
        p_free[bank] = m_z3
        pl.wait(EV, m_z3)
        V = gsc
        m_zg = pl.emit(EV, lambda v: v.tensor_add(
            V[0:1, 0:1], V[0:1, 0:1], gcb3[0:1, 0:1]), inc="dv")
        pl.wait(ES, m_zg)
        pl.emit(ES, lambda s: s.activation(V[0:1, 1:2], V[0:1, 0:1], ACT.Tanh,
                                           scale=0.5), inc="ac")
        m_th = pl.mark("ac")
        pl.wait(EV, m_th)
        pl.emit(EV, lambda v: v.tensor_scalar(
            V[0:1, 1:2], V[0:1, 1:2], 0.5, 0.5, ALU.mult, ALU.add))
        pl.emit(EV, lambda v: v.tensor_scalar(
            V[0:1, 2:3], V[0:1, 1:2], 0.7, -0.5, ALU.mult, ALU.add))
        pl.emit(EV, lambda v: v.tensor_mul(V[0:1, 2:3], V[0:1, 2:3], sc_raw[0:1, 2:3]))
        pl.emit(EV, lambda v: v.tensor_scalar(
            V[0:1, 2:3], V[0:1, 2:3], 2.0, 0.1, ALU.mult, ALU.add))
        pl.emit(EV, lambda v: v.tensor_scalar_mul(V[0:1, 3:4], sc_raw[0:1, 2:3], 0.6))
        pl.emit(EV, lambda v: v.tensor_scalar_mul(V[0:1, 4:5], V[0:1, 3:4], 0.5))
        pl.emit(EV, lambda v: v.tensor_add(V[0:1, 5:6], V[0:1, 2:3], V[0:1, 4:5]))
        m_sc = pl.emit(EV, lambda v: v.engine_nop(), inc="dv")
        bank = 3
        pl.waits(ET, [m_sc, p_free[bank]])
        mm(P[bank][:, 0:1], ones_row[:, :], V[0:1, 5:6], True, False, inc=False)
        mm(P[bank][:, 1:2], ones_row[:, :], V[0:1, 4:5], False, True, inc=True)
        pl.wait(ES, pl.mark("pe"))
        pl.emit(ES, lambda s: s.copy(a2_col[idx][:, :], P[bank][:, 0:1]), inc="ac")
        pl.emit(ES, lambda s: s.copy(c2_col[idx][:, :], P[bank][:, 1:2]), inc="ac")
        m_done = pl.mark("ac")
        p_free[bank] = m_done
        return m_done

    pl.wait(EY, xbuf_free)
    pl.emit(EY, lambda y: y.dma_start(
        out=xbuf[:, 0:TPB, :],
        in_=x_d[0:TB, :].rearrange("(k p) d -> p k d", p=128)),
        inc="xld", inck=16)
    pl.emit(EY, lambda y: y.dma_start(
        out=xbuf[0:6, TPB, :], in_=xh_d), inc="xld", inck=16)
    m_xld0 = pl.mark("xld")
    tl0 = [(t, t, 128) for t in range(TPB)] + [(TPB, HCOL, 6)]
    pl.waits(ES, [m_xld0, aux_free])
    for (t, col, npart) in tl0:
        pl.emit(ES, lambda s, t=t, c=col, n=npart: s.activation(
            aux[0:n, t, :], xbuf[0:n, t, :], ACT.Identity,
            scale=iv1[0:n, c:c + 1], bias=nm1[0:n, c:c + 1]), inc="ac")
    pre2 = (m_xld0, pl.mark("ac"))

    m_gfA = allreduce_gf(0, m_sums1)

    m_sums2 = None
    swg_slot_free = [None] * NRING
    swg_slot_count = [0] * NRING
    wc_slot_free = [None, None]
    wc_slot_count = [0, 0]
    mxs_count = 0
    halo6_ready = setup_done
    scr_mixer_gate = None
    x2st_marks = []

    def ffn_cycles(zc_first, idx_cols, blk, with_halo, m_apply, buf=None):
        if buf is None:
            buf = aux
        col0 = blk * TPB
        m_cp = coupling_from_z(zc_first, col0, col0 + TPB, idx_cols, m_apply)
        m_slice = [None] * NSL
        m_halo = None
        for s_ in range(NSL):
            for q in range(4):
                m_slice[s_] = twist_tile(buf, s_ * 4 + q, 128,
                                         col0 + s_ * 4 + q, m_cp, slot=s_ % 2)
        if with_halo:
            m_cph = coupling_from_z(zc_first, HCOL, HCOL + 1, idx_cols, m_apply)
            m_halo = twist_tile(buf, TPB, 6, HCOL, m_cph)
        for cyc in (2, 3):
            for s_ in range(NSL):
                pl.wait(ET, m_slice[s_])
                for c in range(NCH):
                    bank = c % 2
                    pl.wait(ET, p_free[bank])
                    for q in range(4):
                        transp(P[bank][:, q * 128:(q + 1) * 128],
                               buf[:, s_ * 4 + q, c * 128:(c + 1) * 128],
                               inc=(q == 3), start=(q == 0), stop=(q == 3))
                    pl.wait(ES, pl.mark("pe"))
                    pl.emit(ES, lambda s, c=c, b2=bank: s.activation(
                        outT[:, c, 0:512], P[b2][:, 0:512], ACT.Identity), inc="ac")
                    p_free[bank] = pl.mark("ac")
                m_tr = pl.mark("ac")
                pl.wait(ET, m_tr)
                mz = chain_slice(lambda k: outT[:, k, 0:512])
                mzc = zrow_to_cols(ztc, col0 + s_ * 4, 4, mz)
                m_cp2 = coupling_from_z(ztc, col0 + s_ * 4, col0 + s_ * 4 + 4,
                                        idx_cols, mzc)
                for q in range(4):
                    m_slice[s_] = twist_tile(buf, s_ * 4 + q, 128,
                                             col0 + s_ * 4 + q, m_cp2,
                                             slot=s_ % 2)
            if with_halo:
                pl.wait(ET, m_halo)
                bank = 2
                pl.wait(ET, p_free[bank])
                for c in range(NCH):
                    transp(P[bank][:, c * 64:c * 64 + 6],
                           buf[0:6, TPB, c * 128:(c + 1) * 128], inc=(c == NCH - 1), np_=6,
                           start=(c == 0), stop=(c == NCH - 1))
                pl.wait(ES, pl.mark("pe"))
                def fh(s, b2=bank):
                    src = P[b2][:, 0:NCH * 64].rearrange("p (c o) -> p c o", o=64)
                    return s.copy(outT[:, :, 0:6], src[:, :, 0:6])
                pl.emit(ES, fh, inc="ac")
                p_free[bank] = pl.mark("ac")
                mzh2 = chain_slice(lambda k: outT[:, k, 0:6], width=6)
                mzch = zrow_to_cols(ztc, HCOL, 1, mzh2, last_np=6)
                m_cp3 = coupling_from_z(ztc, HCOL, HCOL + 1, idx_cols, mzch)
                m_halo = twist_tile(buf, TPB, 6, HCOL, m_cp3)
        m_all = [m for m in m_slice + [m_halo] if m]
        last = max(m_all, key=lambda m: m[1])
        return pl.mark("pe"), last, m_slice, m_halo

    for b in range(NBLK):
        halo = (b == 0)
        col0 = b * TPB
        in_state = (b < SBLK)
        tl = [(t, col0 + t, 128) for t in range(TPB)]
        if halo:
            tl.append((TPB, HCOL, 6))

        if b == 0:
            m_xld, m_apply = pre2
        else:
            pl.wait(EY, xbuf_free)
            pl.emit(EY, lambda y, b=b: y.dma_start(
                out=xbuf[:, 0:TPB, :],
                in_=x_d[b * TB:(b + 1) * TB, :].rearrange("(k p) d -> p k d", p=128)),
                inc="xld", inck=16)
            m_xld = pl.mark("xld")
            pl.waits(ES, [m_xld, aux_free])
            for (t, col, npart) in tl:
                pl.emit(ES, lambda s, t=t, c=col, n=npart: s.activation(
                    aux[0:n, t, :], xbuf[0:n, t, :], ACT.Identity,
                    scale=iv1[0:n, c:c + 1], bias=nm1[0:n, c:c + 1]), inc="ac")
            m_apply = pl.mark("ac")

        if stop_after == "lnx":
            if b == 0:
                dbg = nc.dram_tensor("dbg", [128, 5, NT1], F32,
                                     kind="ExternalOutput").ap()
                pl.waits(EY, [m_apply])
                for i, tsr in enumerate((mu1, iv1, nm1, vscr, cscr)):
                    pl.emit(EY, lambda y, i=i, t=tsr: y.dma_start(
                        out=dbg[:, i, :], in_=t[:, :]), inc="xst", inck=16)
            pl.waits(EY, [m_apply])
            m_yst = pl.emit(EY, lambda y, b=b: y.dma_start(
                out=y_d[b * TB:(b + 1) * TB, :].rearrange("(k p) d -> p k d", p=128),
                in_=aux[:, 0:TPB, :]), inc="xst", inck=16)
            xbuf_free = m_apply
            aux_free = m_yst
            scr_mixer_gate = pl.mark("pe")
            continue

        if stop_after == "c1":
            m_cp = coupling_from_z(z1c, col0, col0 + TPB, 0, m_apply)
            m_last = None
            for t in range(TPB):
                m_last = twist_tile(aux, t, 128, col0 + t, m_cp)
            pl.waits(EY, [m_last])
            m_yst = pl.emit(EY, lambda y, b=b: y.dma_start(
                out=y_d[b * TB:(b + 1) * TB, :].rearrange("(k p) d -> p k d", p=128),
                in_=aux[:, 0:TPB, :]), inc="xst", inck=16)
            xbuf_free = m_apply
            aux_free = m_yst
            scr_mixer_gate = pl.mark("pe")
            continue

        m_chain2, m_aux_upd, m_sl_p2, m_halo_p2 = ffn_cycles(
            z1c, 0, b, halo, m_apply)

        if stop_after == "x1":
            pl.wait(EV, m_aux_upd)
            m_x1 = pl.emit(EV, lambda v: v.scalar_tensor_tensor(
                xbuf[:, 0:TPB, :], aux[:, 0:TPB, :], 0.5, xbuf[:, 0:TPB, :],
                ALU.mult, ALU.add), inc="dv", wide=_w('scale'))
            pl.waits(EY, [m_x1])
            m_yst = pl.emit(EY, lambda y, b=b: y.dma_start(
                out=y_d[b * TB:(b + 1) * TB, :].rearrange("(k p) d -> p k d", p=128),
                in_=xbuf[:, 0:TPB, :]), inc="xst", inck=16)
            xbuf_free = m_yst
            aux_free = m_x1
            scr_mixer_gate = m_chain2
            continue

        m_h_sl = [None] * NSL
        for s_ in range(NSL):
            tls = [(s_ * 4 + q, col0 + s_ * 4 + q, 128) for q in range(4)]
            pl.wait(EV, m_sl_p2[s_])
            m_x1s = pl.emit(EV, lambda v, s_=s_: v.scalar_tensor_tensor(
                xbuf[:, s_ * 4:(s_ + 1) * 4, :], aux[:, s_ * 4:(s_ + 1) * 4, :],
                0.5, xbuf[:, s_ * 4:(s_ + 1) * 4, :], ALU.mult, ALU.add),
                inc="dv", wide=_w('scale'))
            if halo and s_ == NSL - 1:
                tls.append((TPB, HCOL, 6))
                pl.wait(EV, m_halo_p2)
                m_x1s = pl.emit(EV, lambda v: v.scalar_tensor_tensor(
                    xbuf[0:6, TPB, :], aux[0:6, TPB, :], 0.5, xbuf[0:6, TPB, :],
                    ALU.mult, ALU.add), inc="dv", wide=_w('scale'))
            pl.wait(ES, m_x1s)
            for (t, col, npart) in tls:
                stats_tile(xbuf[:, t, :], col, mu2, npart)
            if halo and s_ == NSL - 1:
                stat_cols(mu2, iv2, nm2, col0 + s_ * 4, col0 + s_ * 4 + 4,
                          LN_EPS, r_t=rr2)
                m_st2 = stat_cols(mu2, iv2, nm2, HCOL, HCOL + 1, LN_EPS,
                                  r_t=rr2)
            else:
                m_st2 = stat_cols(mu2, iv2, nm2, col0 + s_ * 4,
                                  col0 + s_ * 4 + 4, LN_EPS, r_t=rr2)
            pl.wait(ES, m_st2)
            for (t, col, npart) in tls:
                pl.emit(ES, lambda s, t=t, c=col, n=npart: s.activation(
                    aux[0:n, t, :], xbuf[0:n, t, :], ACT.Identity,
                    scale=iv2[0:n, c:c + 1], bias=nm2[0:n, c:c + 1]), inc="ac")
            m_lnx1s = pl.mark("ac")
            pl.wait(EV, m_lnx1s)
            for (t, col, npart) in tls:
                pl.emit(EV, lambda v, t=t, c=col, n=npart: v.tensor_scalar_mul(
                    aux[0:n, t, :], aux[0:n, t, :], rr2[0:n, c:c + 1]), inc="dv",
                    wide=_w('scale'))
            m_h_sl[s_] = pl.mark("dv")
        m_x1 = m_h_sl[NSL - 1]

        pl.wait(ET, ht_free)
        for s_ in range(NSL):
            pl.wait(ET, m_h_sl[s_])
            for c in range(NCH):
                bank = (s_ * NCH + c) % 2
                pl.wait(ET, p_free[bank])
                for q in range(4):
                    transp(P[bank][:, q * 128:(q + 1) * 128],
                           aux[:, s_ * 4 + q, c * 128:(c + 1) * 128], inc=(q == 3),
                           start=(q == 0), stop=(q == 3))
                pl.wait(ES, pl.mark("pe"))
                pl.emit(ES, lambda s, c=c, s_=s_, b2=bank: s.activation(
                    hT[:, c, 6 + s_ * 512: 6 + (s_ + 1) * 512], P[b2][:, 0:512],
                    ACT.Identity, scale=n1w[:, c:c + 1]), inc="ac")
                p_free[bank] = pl.mark("ac")
        if halo:
            bank = 2
            pl.wait(ET, p_free[bank])
            for c in range(NCH):
                transp(P[bank][:, c * 64:c * 64 + 6],
                       aux[0:6, TPB, c * 128:(c + 1) * 128], inc=(c == NCH - 1), np_=6,
                       start=(c == 0), stop=(c == NCH - 1))
            pl.wait(ES, pl.mark("pe"))
            for c in range(NCH):
                pl.emit(ES, lambda s, c=c, b2=bank: s.activation(
                    hT[:, c, 0:6], P[b2][:, c * 64:c * 64 + 6], ACT.Identity,
                    scale=n1w[:, c:c + 1]), inc="ac")
            p_free[bank] = pl.mark("ac")
            m_ht = pl.mark("ac")
        else:
            pl.waits(EV, [halo6_ready, pl.mark("ac")])
            m_ht = pl.emit(EV, lambda v: v.tensor_copy(
                hT[:, :, 0:6], halo6[:, :, :]), inc="dv")
        pl.waits(EV, [m_ht, pl.mark("ac")])
        halo6_ready = pl.emit(EV, lambda v: v.tensor_copy(
            halo6[:, :, :], hT[:, :, TB:TB + 6]), inc="dv")
        aux_scaled_free = pl.mark("pe")

        pl.waits(EG, [scr_mixer_gate, m_chain2])
        pl.emit(EG, lambda g: g.dma_start(
            out=wk_sl[:, :, :], in_=dr["Wk"].rearrange("(k p) m -> p k m", p=128)),
            inc="mxs", inck=16)
        pl.emit(EG, lambda g: g.dma_start(
            out=mt_sl[:, :, :], in_=dr["Mt"].rearrange("(k p) m -> p k m", p=128)),
            inc="mxs", inck=16)
        pl.emit(EG, lambda g: g.dma_start(
            out=wme_sl[:, :, :], in_=dr["Wmem"].rearrange("(k p) m -> p k m", p=128)),
            inc="mxs", inck=16)
        m_mixw = pl.mark("mxs")

        if in_state:
            m_state_done = None
            m_state_prev = None
            for half in range(2):
                pl.waits(EG, [scr_mixer_gate, m_chain2])
                if half == 1:
                    pl.waits(EG, [m_half_mm])
                pl.emit(EG, lambda g, h=half: g.dma_start(
                    out=wdg_h[:, 0, :, :, :],
                    in_=dr["Wdecay_t"][4 * h:4 * h + 4].rearrange(
                        "o k p m -> p o k m")), inc="mxs", inck=16)
                pl.emit(EG, lambda g, h=half: g.dma_start(
                    out=wdg_h[:, 1, :, :, :],
                    in_=dr["Wgate_t"][4 * h:4 * h + 4].rearrange(
                        "o k p m -> p o k m")), inc="mxs", inck=16)
                m_wdg = pl.mark("mxs")
                for ci in range(4):
                    c = half * 4 + ci
                    for s2 in range(NSL):
                        hs = lambda k: hT[:, k, 6 + s2 * 512: 6 + (s2 + 1) * 512]
                        bank = 0
                        pl.waits(ET, [m_wdg, m_ht, p_free[bank]])
                        for k in range(NCH):
                            mm(P[bank][:, 0:512], wdg_h[:, 0, ci, k, :], hs(k),
                               k == 0, k == NCH - 1, inc=(k == NCH - 1))
                        pl.waits(ES, [pl.mark("pe"), m_state_prev])
                        pl.emit(ES, lambda s, c=c, b2=bank: s.activation(
                            st_scr[0][:, :], P[b2][:, 0:512], ACT.Sigmoid,
                            bias=bdec[:, c:c + 1]), inc="ac")
                        m_dec = pl.mark("ac")
                        p_free[bank] = m_dec
                        bank = 1
                        pl.wait(ET, p_free[bank])
                        for k in range(NCH):
                            mm(P[bank][:, 0:512], wdg_h[:, 1, ci, k, :], hs(k),
                               k == 0, k == NCH - 1, inc=(k == NCH - 1))
                        pl.wait(ES, pl.mark("pe"))
                        pl.emit(ES, lambda s, c=c, b2=bank: s.activation(
                            st_scr[3][:, :], P[b2][:, 0:512], ACT.Sigmoid,
                            bias=bgat[:, c:c + 1]), inc="ac")
                        m_gat = pl.mark("ac")
                        p_free[bank] = m_gat
                        pl.wait(EV, m_dec)
                        m_clip = pl.emit(EV, lambda v: v.tensor_scalar(
                            st_scr[0][:, :], st_scr[0][:, :], 0.05, 0.95,
                            ALU.max, ALU.min), inc="dv", wide=_w('state'))
                        pl.wait(ES, m_clip)
                        pl.emit(ES, lambda s: s.activation(
                            st_scr[1][:, :], st_scr[0][:, :], ACT.Ln), inc="ac")
                        m_ldv = pl.mark("ac")
                        pl.wait(EV, m_ldv)
                        pl.emit(EV, lambda v, c=c: v.tensor_tensor_scan(
                            st_scr[2][:, :], st_scr[1][:, :], st_scr[1][:, :],
                            lcd_carry[:, c:c + 1], ALU.add, ALU.bypass), wide=_w('state'))
                        pl.emit(EV, lambda v, c=c: v.tensor_copy(
                            lcd_carry[:, c:c + 1], st_scr[2][:, 511:512]))
                        pl.emit(EV, lambda v: v.tensor_sub(
                            st_scr[1][:, :], st_scr[1][:, :], st_scr[2][:, :]), wide=_w('state'))
                        m_ze = pl.emit(EV, lambda v: v.tensor_scalar_min(
                            st_scr[1][:, :], st_scr[1][:, :], float(EXP_CLAMP)),
                            inc="dv", wide=_w('state'))
                        pl.wait(ES, m_ze)
                        pl.emit(ES, lambda s: s.activation(
                            st_scr[0][:, :], st_scr[1][:, :], ACT.Exp), inc="ac")
                        m_e1 = pl.mark("ac")
                        pl.waits(EV, [m_e1, m_gat])
                        pl.emit(EV, lambda v: v.tensor_scalar_min(
                            st_scr[0][:, :], st_scr[0][:, :], float(RECIP_MAX)), wide=_w('state'))
                        pl.emit(EV, lambda v, c=c, s2=s2: v.tensor_mul(
                            st_scr[3][:, :], st_scr[3][:, :],
                            hT[:, c, 6 + s2 * 512: 6 + (s2 + 1) * 512]), wide=_w('state'))
                        pl.emit(EV, lambda v: v.tensor_mul(
                            st_scr[3][:, :], st_scr[3][:, :], st_scr[0][:, :]), wide=_w('state'))
                        pl.emit(EV, lambda v, c=c: v.tensor_tensor_scan(
                            st_scr[1][:, :], st_scr[3][:, :], st_scr[3][:, :],
                            cs2_carry[:, c:c + 1], ALU.add, ALU.bypass), wide=_w('state'))
                        m_cs2 = pl.emit(EV, lambda v, c=c: v.tensor_copy(
                            cs2_carry[:, c:c + 1], st_scr[1][:, 511:512]), inc="dv")
                        pl.wait(ES, m_cs2)
                        pl.emit(ES, lambda s: s.activation(
                            st_scr[0][:, :], st_scr[2][:, :], ACT.Exp), inc="ac")
                        m_e2 = pl.mark("ac")
                        pl.wait(EV, m_e2)
                        m_state_done = pl.emit(
                            EV, lambda v, c=c, s2=s2: v.scalar_tensor_tensor(
                                h2T[:, c, s2 * 512:(s2 + 1) * 512], st_scr[1][:, :],
                                ssm_col[:, 0:1], st_scr[0][:, :],
                                ALU.mult, ALU.mult), inc="dv", wide=_w('state'))
                        m_state_prev = m_state_done
                m_half_mm = pl.mark("pe")
        else:
            m_state_done = None

        for s2 in range(NSL):
            bank = 0
            for o2 in range(2):
                pl.waits(ET, [m_mixw, m_ht, p_free[bank]])
                for k in range(NCH):
                    mm(P[bank][:, 0:512], wk_sl[:, k, o2 * 128:(o2 + 1) * 128],
                       hT[:, k, 6 + s2 * 512: 6 + (s2 + 1) * 512],
                       k == 0, k == NCH - 1, inc=(k == NCH - 1))
                pl.wait(ES, pl.mark("pe"))
                pl.emit(ES, lambda s, o2=o2, b2=bank: s.copy(
                    kT_s[:, o2, :], P[b2][:, 0:512]), inc="ac")
                p_free[bank] = pl.mark("ac")
            m_kT = pl.mark("ac")
            bank = 1
            for o2 in range(2):
                pl.waits(ET, [m_kT, p_free[bank]])
                for k in range(2):
                    mm(P[bank][:, 0:512], mt_sl[:, k, o2 * 128:(o2 + 1) * 128],
                       kT_s[:, k, :], k == 0, k == 1, inc=(k == 1))
                pl.wait(ES, pl.mark("pe"))
                pl.emit(ES, lambda s, o2=o2, b2=bank: s.copy(
                    moT_s[:, o2, :], P[b2][:, 0:512]), inc="ac")
                p_free[bank] = pl.mark("ac")
            m_moT = pl.mark("ac")
            for o in range(NCH):
                bank = o % 2
                pl.waits(ET, [m_moT, p_free[bank]])
                for k in range(2):
                    mm(P[bank][:, 0:512], wme_sl[:, k, o * 128:(o + 1) * 128],
                       moT_s[:, k, :], k == 0, k == 1, inc=(k == 1))
                pl.wait(ES, pl.mark("pe"))
                pl.emit(ES, lambda s, o=o, s2=s2, b2=bank: s.activation(
                    memT_b[:, o, s2 * 512:(s2 + 1) * 512], P[b2][:, 0:512],
                    ACT.Identity, scale=ms_col[:, 0:1]), inc="ac")
                p_free[bank] = pl.mark("ac")
        m_memT = pl.mark("ac")

        m_local = None
        if in_state:
            cv_slots = [sqs[:, 0:512], sqs[:, 512:1024]]
            depth = 1
        else:
            h2f = h2T[:, :, :].rearrange("p a b -> p (a b)").bitcast(F32)
            cv_slots = ([h2f[:, i * 512:(i + 1) * 512] for i in range(8)]
                        + [sqs[:, 0:512], sqs[:, 512:1024]])
            depth = 5
        nslot = len(cv_slots)
        first_copy = True

        def cv_copy(cc):
            nonlocal m_local, first_copy
            if first_copy:
                pl.waits(EV, [m_memT, m_state_done])
                first_copy = False
            for s2 in range(NSL):
                d = cv_slots[(cc * NSL + s2) % nslot]
                m_local = pl.emit(EV, lambda v, cc=cc, s2=s2, d=d:
                                  v.tensor_copy(
                                      hT[:, cc, 6 + s2 * 512: 6 + (s2 + 1) * 512],
                                      d[:, :]), inc="dv", wide=_w('conv'))

        for c in range(NCH):
            if c >= depth:
                cv_copy(c - depth)
            pl.wait(EV, m_ht)
            for s2 in range(NSL):
                d = cv_slots[(c * NSL + s2) % nslot]
                pl.emit(EV, lambda v, c=c, s2=s2, d=d: v.tensor_scalar_mul(
                    d[:, :], hT[:, c, s2 * 512 + 0: s2 * 512 + 512],
                    convw[:, c, 0:1]), wide=_w('conv'))
                for j in range(1, 7):
                    pl.emit(EV, lambda v, c=c, s2=s2, j=j, d=d:
                            v.scalar_tensor_tensor(
                                d[:, :], hT[:, c, s2 * 512 + j: s2 * 512 + 512 + j],
                                convw[:, c, j:j + 1], d[:, :], ALU.mult, ALU.add),
                            wide=_w('conv'))
        for cc in range(NCH - depth, NCH):
            cv_copy(cc)

        m_mix_done_parts = []
        m_rebuild = None
        for o in range(NCH):
            slot = o % 2
            sem = wc_sem[slot]
            pl.waits(EG, [wc_slot_free[slot], scr_mixer_gate, m_chain2, m_memT])
            if in_state:
                pl.emit(EG, lambda g, o=o, sl=slot: g.dma_start(
                    out=wc_ring[sl][:, :, :],
                    in_=dr["Wc_t"][o].rearrange("k p m -> p k m")),
                    inc=sem, inck=16)
                wc_slot_count[slot] += 1
            else:
                pl.emit(EG, lambda g, o=o, sl=slot: g.dma_start(
                    out=wc_ring[sl][:, 0:NCH, :],
                    in_=dr["Wc_t"][o, 0:NCH].rearrange("k p m -> p k m")),
                    inc=sem, inck=16)
                pl.emit(EG, lambda g, o=o, sl=slot: g.dma_start(
                    out=wc_ring[sl][:, 2 * NCH:3 * NCH, :],
                    in_=dr["Wc_t"][o, 2 * NCH:3 * NCH].rearrange("k p m -> p k m")),
                    inc=sem, inck=16)
                wc_slot_count[slot] += 2
            m_wcl = (sem, wc_slot_count[slot] * 16)
            wc_parts = {"mixed_L": "L", "mixed_S": "S", "mixed_M": "M"}.get(
                stop_after, "LSM")
            for s2 in range(NSL):
                bank = s2
                pl.waits(ET, [m_wcl, p_free[bank]])
                plan_mm = []
                if "M" in wc_parts:
                    for k in range(NCH):
                        plan_mm.append((wc_ring[slot][:, 2 * NCH + k, :],
                                        memT_b[:, k, s2 * 512:(s2 + 1) * 512], None))
                if "S" in wc_parts and in_state:
                    for k in range(NCH):
                        plan_mm.append((wc_ring[slot][:, NCH + k, :],
                                        h2T[:, k, s2 * 512:(s2 + 1) * 512],
                                        m_state_done if k == 0 else None))
                if "L" in wc_parts:
                    for k in range(NCH):
                        plan_mm.append((wc_ring[slot][:, k, :],
                                        hT[:, k, 6 + s2 * 512: 6 + (s2 + 1) * 512],
                                        m_local if k == 0 else None))
                for i, (lh, rh, mw) in enumerate(plan_mm):
                    if mw is not None:
                        pl.wait(ET, mw)
                    mm(P[bank][:, 0:512], lh, rh, i == 0,
                       i == len(plan_mm) - 1, inc=(i == len(plan_mm) - 1))
                pl.wait(ES, pl.mark("pe"))
                pl.emit(ES, lambda s, s2=s2, o=o, b2=bank: s.activation(
                    combT[s2][:, :], P[b2][:, 0:512], ACT.Identity,
                    bias=bc_c[:, o:o + 1]), inc="ac")
                m_cmb = pl.mark("ac")
                p_free[bank] = m_cmb
                bank2 = 2 + (s2 % 2)
                pl.waits(ET, [m_cmb, p_free[bank2], aux_scaled_free])
                for q in range(4):
                    transp(P[bank2][:, q * 128:(q + 1) * 128],
                           combT[s2][:, q * 128:(q + 1) * 128],
                           inc=(q == 3), start=(q == 0), stop=(q == 3))
                m_trc = pl.mark("pe")
                if o == 0:
                    pl.wait(ES, m_trc)
                    for q in range(4):
                        t = s2 * 4 + q
                        pl.emit(ES, lambda s, t=t, c=col0 + t: s.activation(
                            aux[:, t, :], xbuf[:, t, :], ACT.Identity,
                            scale=iv2[:, c:c + 1], bias=nm2[:, c:c + 1]), inc="ac")
                    m_rebuild = pl.mark("ac")
                pl.waits(EV, [m_trc, m_rebuild])
                m_add = pl.emit(EV, lambda v, s2=s2, o=o, b2=bank2: v.tensor_add(
                    aux[:, s2 * 4:(s2 + 1) * 4, o * 128:(o + 1) * 128],
                    aux[:, s2 * 4:(s2 + 1) * 4, o * 128:(o + 1) * 128],
                    P[b2][:, 0:512].rearrange("p (q w) -> p q w", w=128)), inc="dv",
                    wide=_w('misc'))
                p_free[bank2] = m_add
            wc_slot_free[slot] = pl.mark("pe")
        m_mixed = pl.mark("dv")
        m_mixer_mm_done = pl.mark("pe")

        if stop_after in ("mixed", "mixed_L", "mixed_S", "mixed_M"):
            pl.waits(EY, [m_mixed])
            m_yst = pl.emit(EY, lambda y, b=b: y.dma_start(
                out=y_d[b * TB:(b + 1) * TB, :].rearrange("(k p) d -> p k d", p=128),
                in_=aux[:, 0:TPB, :]), inc="xst", inck=16)
            xbuf_free = m_yst
            aux_free = m_yst
            scr_mixer_gate = m_mixer_mm_done
            continue

        pl.wait(EV, m_mixed)
        pl.wait(ES, m_mixed)
        for t in range(TPB):
            stats_tile(aux[:, t, :], col0 + t, mu1)
        m_r3 = stat_cols(mu1, None, None, col0, col0 + TPB, 0.0, r_t=rr3,
                         r_only=True)
        m_x2a = pl.emit(EV, lambda v: v.tensor_add(
            xbuf[:, 0:TPB, :], xbuf[:, 0:TPB, :], aux[:, 0:TPB, :]), inc="dv",
            wide=_w('scale'))

        pl.waits(EV, [m_r3, m_x2a])
        for t in range(TPB):
            pl.emit(EV, lambda v, t=t, c=col0 + t: v.tensor_scalar_mul(
                aux[:, t, :], aux[:, t, :], rr3[:, c:c + 1]), inc="dv", wide=_w('scale'))
        m_scl = pl.mark("dv")
        pl.waits(ET, [m_scl])
        for s_ in range(NSL):
            for c in range(NCH):
                bank = (s_ * NCH + c) % 2
                pl.wait(ET, p_free[bank])
                for q in range(4):
                    transp(P[bank][:, q * 128:(q + 1) * 128],
                           aux[:, s_ * 4 + q, c * 128:(c + 1) * 128], inc=(q == 3),
                           start=(q == 0), stop=(q == 3))
                pl.wait(ES, pl.mark("pe"))
                pl.emit(ES, lambda s, c=c, s_=s_, b2=bank: s.activation(
                    h2T[:, c, s_ * 512:(s_ + 1) * 512], P[b2][:, 0:512],
                    ACT.Identity, scale=n2w[:, c:c + 1]), inc="ac")
                p_free[bank] = pl.mark("ac")
        m_h2T = pl.mark("ac")
        aux_ff_free = pl.mark("pe")

        ff_mark = {}
        pf_free4 = [None] * 4
        m_prev_pr = [None, None]
        m_prev_wdown = None
        import os as _os
        _NG = int(_os.environ.get('SWG_GROUPS', '4'))
        for g_ in range(_NG):
            for st in range(NST):
                for hh in range(NCH):
                    h = g_ * NCH + hh
                    slot = h % NRING
                    sem = swg_sem[slot]
                    if st == 0:
                        pl.waits(EG, [swg_slot_free[slot], m_mixer_mm_done])
                        pl.emit(EG, lambda g, h=h, sl=slot: g.dma_start(
                            out=swg_wg[sl][:, :, :],
                            in_=dr["Wg_t"][h].rearrange("(k p) m -> p k m", p=128)),
                            inc=sem, inck=16)
                        pl.emit(EG, lambda g, h=h, sl=slot: g.dma_start(
                            out=swg_wu[sl][:, :, :],
                            in_=dr["Wup_t"][h].rearrange("(k p) m -> p k m", p=128)),
                            inc=sem, inck=16)
                        pl.emit(EG, lambda g, h=h, sl=slot: g.dma_start(
                            out=swg_wd[sl][:, :, :],
                            in_=dr["Wdown"][h * 128:(h + 1) * 128, :]
                            .rearrange("p (a b) -> p a b", b=128)),
                            inc=sem, inck=16)
                        swg_slot_count[slot] += 3
                    m_wl = (sem, swg_slot_count[slot] * 16)
                    bg = 2 * (hh % 2)
                    bu = bg + 1
                    pl.waits(ET, [m_wl, m_h2T, p_free[bg], p_free[bu]])
                    for k in range(NCH):
                        mm(P[bg][:, 0:256], swg_wg[slot][:, k, :],
                           h2T[:, k, st * 256:(st + 1) * 256],
                           k == 0, k == NCH - 1, inc=False)
                    for k in range(NCH):
                        mm(P[bu][:, 0:256], swg_wu[slot][:, k, :],
                           h2T[:, k, st * 256:(st + 1) * 256],
                           k == 0, k == NCH - 1, inc=(k == NCH - 1))
                    m_gu = pl.mark("pe")
                    pl.waits(ES, [m_gu, m_prev_pr[hh % 2]])
                    pl.emit(ES, lambda s, b2=bg, pb=hh % 2: s.activation(
                        silu_sb[pb][:, :], P[b2][:, 0:256], ACT.Silu), inc="ac")
                    m_si = pl.mark("ac")
                    pl.waits(EV, [m_si, m_prev_wdown])
                    m_pr = pl.emit(EV, lambda v, b2=bu, hh=hh, pb=hh % 2:
                                   v.tensor_mul(prod_buf[:, hh, :],
                                                silu_sb[pb][:, :],
                                                P[b2][:, 0:256]), inc="dv",
                                   wide=_w('misc'))
                    m_prev_pr[hh % 2] = m_pr
                    p_free[bg] = m_pr
                    p_free[bu] = m_pr
                m_prods = pl.mark("dv")
                for o in range(NCH):
                    pfb = o % 4
                    region = PF[:, pfb * 512: pfb * 512 + 256]
                    pl.waits(ET, [m_prods, pf_free4[pfb]])
                    for hh in range(NCH):
                        slot = (g_ * NCH + hh) % NRING
                        mm(region, swg_wd[slot][:, o, :], prod_buf[:, hh, :],
                           hh == 0, hh == NCH - 1, inc=(hh == NCH - 1))
                    m_pf = pl.mark("pe")
                    key = (st, o)
                    if g_ == 0:
                        pl.waits(ES, [m_pf, aux_ff_free])
                        def fcp(s, st=st, o=o, r=region):
                            return s.copy(aux[:, o, st * 256:(st + 1) * 256], r)
                        ff_mark[key] = pl.emit(ES, fcp, inc="ac")
                    else:
                        pl.waits(EV, [m_pf, ff_mark[key]])
                        def fad(v, st=st, o=o, r=region):
                            return v.tensor_add(
                                aux[:, o, st * 256:(st + 1) * 256],
                                aux[:, o, st * 256:(st + 1) * 256], r)
                        ff_mark[key] = pl.emit(EV, fad, inc="dv", wide=_w('misc'))
                    pf_free4[pfb] = ff_mark[key]
                m_prev_wdown = pl.mark("pe")
            for hh in range(NCH):
                slot = (g_ * NCH + hh) % NRING
                swg_slot_free[slot] = pl.mark("pe")
        m_ff = pl.mark("dv")
        m_swg_done = pl.mark("pe")

        if stop_after == "swg":
            pl.waits(EY, [m_ff, pl.mark("ac"), m_swg_done])
            m_yst = pl.emit(EY, lambda y, b=b: y.dma_start(
                out=y_d[b * TB:(b + 1) * TB, :].rearrange("(k p) d -> p k d", p=128),
                in_=xbuf[:, 0:TPB, :]), inc="xst", inck=16)
            xbuf_free = m_yst
            aux_free = m_yst
            scr_mixer_gate = m_swg_done
            continue

        def ff_back_slice(s2):
            best = {}
            for st in (2 * s2, 2 * s2 + 1):
                for o in range(NCH):
                    sem, cnt = ff_mark[(st, o)]
                    best[sem] = max(best.get(sem, 0), cnt)
            gates = list(best.items())
            m_addf = None
            for o in range(NCH):
                bank = 2 + ((s2 * NCH + o) % 2)
                pl.waits(ET, gates + [p_free[bank]])
                for q in range(4):
                    transp(P[bank][:, q * 128:(q + 1) * 128],
                           aux[:, o, s2 * 512 + q * 128: s2 * 512 + (q + 1) * 128],
                           inc=(q == 3), start=(q == 0), stop=(q == 3))
                m_trf = pl.mark("pe")
                pl.wait(EV, m_trf)
                m_addf = pl.emit(EV, lambda v, s2=s2, o=o, b2=bank: v.tensor_add(
                    xbuf[:, s2 * 4:(s2 + 1) * 4, o * 128:(o + 1) * 128],
                    xbuf[:, s2 * 4:(s2 + 1) * 4, o * 128:(o + 1) * 128],
                    P[b2][:, 0:512].rearrange("p (q w) -> p q w", w=128)), inc="dv")
                p_free[bank] = m_addf
            return m_addf

        if stop_after == "x2":
            for s2 in range(NSL):
                ff_back_slice(s2)
            m_x2 = pl.mark("dv")
            pl.wait(EY, m_x2)
            m_yst = pl.emit(EY, lambda y, b=b: y.dma_start(
                out=y_d[b * TB:(b + 1) * TB, :].rearrange("(k p) d -> p k d", p=128),
                in_=xbuf[:, 0:TPB, :]), inc="xst", inck=16)
            xbuf_free = m_yst
            aux_free = pl.mark("pe")
            scr_mixer_gate = m_swg_done
            continue

        m_adds = [ff_back_slice(s2) for s2 in range(NSL)]
        aux_free = pl.mark("pe")

        m_x2st = None
        for s2 in range(NSL):
            c0s = col0 + s2 * 4
            pl.wait(EY, m_adds[s2])
            m_x2st = pl.emit(EY, lambda y, b=b, s2=s2: y.dma_start(
                out=y_d[b * TB + s2 * 512:b * TB + (s2 + 1) * 512, :]
                .rearrange("(k p) d -> p k d", p=128),
                in_=xbuf[:, s2 * 4:(s2 + 1) * 4, :]), inc="xst", inck=16)
            pl.wait(ES, m_adds[s2])
            pl.wait(EV, m_adds[s2])
            for q in range(4):
                stats_tile(xbuf[:, s2 * 4 + q, :], c0s + q, mu1)
            m_st3 = stat_cols(mu1, iv1, nm1, c0s, c0s + 4, LN_EPS)
            pl.waits(ES, [m_st3, m_x2st])
            for q in range(4):
                pl.emit(ES, lambda s, t=s2 * 4 + q, c=c0s + q: s.activation(
                    xbuf[:, t, :], xbuf[:, t, :], ACT.Identity,
                    scale=iv1[:, c:c + 1], bias=nm1[:, c:c + 1]), inc="ac")
            m_lnx2 = pl.mark("ac")
            pl.waits(ET, [m_lnx2, m_swg_done])
            for c in range(NCH):
                bank = c % 2
                pl.wait(ET, p_free[bank])
                for q in range(4):
                    transp(P[bank][:, q * 128:(q + 1) * 128],
                           xbuf[:, s2 * 4 + q, c * 128:(c + 1) * 128], inc=(q == 3),
                           start=(q == 0), stop=(q == 3))
                pl.waits(ES, [pl.mark("pe"), accb_free])
                pl.emit(ES, lambda s, c=c, s2=s2, b2=bank: s.activation(
                    outT[:, c, 0:512], P[b2][:, 0:512], ACT.Identity,
                    accum_out=accb[:, c, s2:s2 + 1]), inc="ac")
                p_free[bank] = pl.mark("ac")
            m_tr25 = pl.mark("ac")
            pl.wait(ET, m_tr25)
            mz = chain_slice(lambda k: outT[:, k, 0:512])
            zrow_to_cols(z1pc, c0s, 4, mz)
        x2st_marks.append(m_x2st)
        m_evac25 = pl.mark("ac")
        pl.wait(EV, m_evac25)
        pl.emit(EV, lambda v: v.tensor_reduce(
            accb[:, :, 0:1], accb[:, :, :], AX.X, ALU.add))
        m_sums2 = pl.emit(EV, lambda v: v.tensor_add(
            sums[1][:, :], sums[1][:, :], accb[:, :, 0]), inc="dv")
        accb_free = m_sums2
        scr_mixer_gate = pl.mark("pe")
        xbuf_free = pl.mark("pe")

    if stop_after is not None:
        pl.wait(EY, pl.mark("xst"))
        with nc.Block() as block:
            @block.sync
            def _(eng):
                for f in pl.q["sync"]:
                    f(eng)

            @block.tensor
            def _(eng):
                for f in pl.q["tensor"]:
                    f(eng)

            @block.vector
            def _(eng):
                for f in pl.q["vector"]:
                    f(eng)

            @block.scalar
            def _(eng):
                for f in pl.q["scalar"]:
                    f(eng)

            @block.gpsimd
            def _(eng):
                for f in pl.q["gpsimd"]:
                    f(eng)
        es.close()
        return nc

    bufs3 = [xbuf, aux]
    buf3_free = [xbuf_free, aux_free]

    def p3_load_apply(b):
        B3 = bufs3[b % 2]
        pl.waits(EY, [buf3_free[b % 2], x2st_marks[b]])
        pl.emit(EY, lambda y, b=b, B3=B3: y.dma_start(
            out=B3[:, 0:TPB, :],
            in_=y_d[b * TB:(b + 1) * TB, :].rearrange("(k p) d -> p k d", p=128)),
            inc="xld", inck=16)
        m_xld = pl.mark("xld")
        pl.wait(ES, m_xld)
        for t in range(TPB):
            pl.emit(ES, lambda s, t=t, c=b * TPB + t, B3=B3: s.activation(
                B3[:, t, :], B3[:, t, :], ACT.Identity,
                scale=iv1[:, c:c + 1], bias=nm1[:, c:c + 1]), inc="ac")
        return pl.mark("ac")

    pre3 = {b: p3_load_apply(b) for b in range(min(2, NBLK))}

    m_gfB = allreduce_gf(1, m_sums2)

    for b in range(NBLK):
        col0 = b * TPB
        B3 = bufs3[b % 2]
        m_apply = pre3[b] if b in pre3 else p3_load_apply(b)
        m_chain3, m_aux_upd, _, _ = ffn_cycles(z1pc, 1, b, False, m_apply,
                                               buf=B3)
        pl.wait(EV, m_aux_upd)
        m_y = pl.emit(EV, lambda v, B3=B3: v.tensor_scalar_mul(
            B3[:, 0:TPB, :], B3[:, 0:TPB, :], 0.5), inc="dv", wide=_w('scale'))
        pl.waits(EG, [m_y, x2st_marks[b]])
        m_yac = pl.emit(EG, lambda g, b=b, B3=B3: g.dma_start(
            out=y_d[b * TB:(b + 1) * TB, :].rearrange("(k p) d -> p k d", p=128),
            in_=B3[:, 0:TPB, :], accum_op=ALU.add), inc="yac", inck=16)
        buf3_free[b % 2] = m_yac

    pl.wait(EY, pl.mark("xst"))
    pl.wait(EY, pl.mark("yac"))

    with nc.Block() as block:
        @block.sync
        def _(eng):
            for f in pl.q["sync"]:
                f(eng)

        @block.tensor
        def _(eng):
            for f in pl.q["tensor"]:
                f(eng)

        @block.vector
        def _(eng):
            for f in pl.q["vector"]:
                f(eng)

        @block.scalar
        def _(eng):
            for f in pl.q["scalar"]:
                f(eng)

        @block.gpsimd
        def _(eng):
            for f in pl.q["gpsimd"]:
                f(eng)

    es.close()
    return nc


_BUILD_CACHE = {}


def _get_program(key):
    if key not in _BUILD_CACHE:
        TC, TB, stt, seq = key
        _BUILD_CACHE[key] = build_program(TC, TB, stt, seq)
    return _BUILD_CACHE[key]


def _prep_weights(inputs):
    F = np.float32

    def g(k):
        return np.ascontiguousarray(np.asarray(inputs[k], F))

    w = {}
    w["conv_w"] = g("L0_attn_conv_w")
    w["norm1_w"] = g("L0_attn_norm1_w")
    w["norm2_w"] = g("L0_attn_norm2_w")
    w["bdecay"] = g("L0_attn_bdecay")
    w["bgate"] = g("L0_attn_bgate")
    w["bc"] = g("L0_attn_bc")
    for i in range(1, 5):
        w[f"cn_w{i}"] = g(f"L0_ffn_cn_w{i}")
        w[f"cn_b{i}"] = g(f"L0_ffn_cn_b{i}")
    for i in range(1, 4):
        w[f"gc_w{i}"] = g(f"L0_ffn_gc_w{i}")
        w[f"gc_b{i}"] = g(f"L0_ffn_gc_b{i}")
    w["Wk"] = g("L0_attn_Wk")
    w["Mt"] = np.ascontiguousarray(g("L0_attn_M").T)
    w["Wmem"] = g("L0_attn_Wmem")
    w["Wdecay_t"] = np.ascontiguousarray(
        g("L0_attn_Wdecay").reshape(NCH, 128, NCH, 128).transpose(2, 0, 1, 3))
    w["Wgate_t"] = np.ascontiguousarray(
        g("L0_attn_Wgate").reshape(NCH, 128, NCH, 128).transpose(2, 0, 1, 3))
    w["Wc_t"] = np.ascontiguousarray(
        g("L0_attn_Wc").reshape(3 * NCH, 128, NCH, 128).transpose(2, 0, 1, 3))
    w["Wg_t"] = np.ascontiguousarray(
        g("L0_attn_Wg").reshape(D, 32, 128).transpose(1, 0, 2))
    w["Wup_t"] = np.ascontiguousarray(
        g("L0_attn_Wup").reshape(D, 32, 128).transpose(1, 0, 2))
    w["Wdown"] = g("L0_attn_Wdown")
    w["state_scale"] = np.asarray(inputs["L0_attn_state_scale"], F).reshape(1)
    w["mem_scale"] = np.asarray(inputs["L0_attn_mem_scale"], F).reshape(1)
    w["ar"] = np.asarray(inputs["L0_ffn_adaptive_range"], F).reshape(1)
    return w


def make_in_maps(inputs, TC):
    x = np.ascontiguousarray(np.asarray(inputs["L0_attn_x"], np.float32))
    w = _prep_weights(inputs)
    nchunk = x.shape[1] // TC
    in_maps = []
    for c in range(NCORE):
        b, k = c // nchunk, c % nchunk
        m = dict(w)
        m["x"] = np.ascontiguousarray(x[b, k * TC:(k + 1) * TC])
        m["xhalo"] = (np.ascontiguousarray(x[b, k * TC - 6:k * TC]) if k > 0
                      else np.zeros((6, D), np.float32))
        m["bm"] = np.ascontiguousarray(np.eye(2, dtype=np.float32)[b])
        m["smask"] = np.asarray([1.0 if k == 0 else 0.0], np.float32)
        in_maps.append(m)
    return in_maps


def kernel(**inputs):
    x = np.asarray(inputs["L0_attn_x"])
    B, T, _ = x.shape
    TC = T // 4
    nc = _get_program((TC, 1024, 2048, T))
    in_maps = make_in_maps(inputs, TC)
    res = run_bass_kernel_spmd(nc, in_maps, core_ids=list(range(NCORE)))
    y = np.zeros((B, T, D), np.float32)
    for c in range(NCORE):
        b, k = c // 4, c % 4
        y[b, k * TC:(k + 1) * TC] = res.results[c]["y"]
    return y


# version 13 (final)
# speedup vs baseline: 1.0089x; 1.0089x over previous
import numpy as np
import ml_dtypes
import concourse.bass as bass
import concourse.mybir as mybir
from concourse.bass_utils import run_bass_kernel_spmd

F32 = mybir.dt.float32
BF16 = mybir.dt.bfloat16
ALU = mybir.AluOpType
ACT = mybir.ActivationFunctionType
AX = mybir.AxisListType

D = 1024
NCH = 8
NCORE = 8
LN_EPS = 1e-5
RMS_EPS = 1e-6
EXP_CLAMP = 16.2
RECIP_MAX = 1e7

ENGINES = ("sync", "tensor", "vector", "scalar", "gpsimd")
import os as _os
WIDE_GROUPS = set(_os.environ.get("WIDE_GROUPS", "twist,conv,misc").split(","))


def _w(group):
    return group in WIDE_GROUPS
ET, EV, ES, EG, EY = "tensor", "vector", "scalar", "gpsimd", "sync"


class Plan:
    def __init__(self, nc):
        self.nc = nc
        self.q = {e: [] for e in ENGINES}
        self.sem_handles = {}
        self.cnt = {}
        self.lastwait = {}

    def sem(self, name):
        if name not in self.sem_handles:
            self.sem_handles[name] = self.nc.semaphore(name).__enter__()
            self.cnt[name] = 0
        return self.sem_handles[name]

    def emit(self, eng, fn, inc=None, inck=1, wide=False):
        if eng == "vector":
            if getattr(self, "_last_v_narrow", False):
                self.wait(eng, getattr(self, "_last_v", None))
            if inc is None and not wide:
                inc = "dvo"
        h = self.sem(inc) if inc else None
        if inc:
            self.cnt[inc] += inck
            mark = (inc, self.cnt[inc])
        else:
            mark = None

        def closure(e, fn=fn, h=h, inck=inck, do=inc is not None):
            ins = fn(e)
            if do:
                assert ins is not None
                ins.then_inc(h, inck)
        self.q[eng].append(closure)
        if eng == "vector":
            if mark is not None:
                self._last_v = mark
            self._last_v_narrow = not wide
        return mark

    def wait(self, eng, mark):
        if mark is None:
            return
        sem, val = mark
        if val <= 0:
            return
        h = self.sem(sem)
        key = (eng, sem)
        if self.lastwait.get(key, 0) >= val:
            return
        self.lastwait[key] = val

        def closure(e, h=h, val=val):
            e.wait_ge(h, val)
        self.q[eng].append(closure)

    def waits(self, eng, marks):
        for m in marks:
            self.wait(eng, m)

    def mark(self, sem):
        self.sem(sem)
        return (sem, self.cnt[sem])


def build_program(TC=4096, TB=1024, state_tokens=2048, seq_total=16384, stop_after=None):
    NBLK = TC // TB
    TPB = TB // 128
    NSL = TB // 512
    NST = TB // 256
    SBLK = max(1, min(NBLK, state_tokens // TB))
    HCOL = NBLK * TPB

    nc = bass.Bass("TRN2", target_bir_lowering=False, debug=False,
                   detect_race_conditions=False)

    dr = {}

    def din(name, shape, dt=F32):
        dr[name] = nc.dram_tensor(name, shape, dt, kind="ExternalInput").ap()
        return dr[name]

    x_d = din("x", [TC, D])
    xh_d = din("xhalo", [6, D])
    din("bm", [2]); din("smask", [1])
    din("state_scale", [1]); din("mem_scale", [1]); din("ar", [1])
    din("conv_w", [D, 7]); din("norm1_w", [D]); din("norm2_w", [D])
    din("bdecay", [D]); din("bgate", [D]); din("bc", [D])
    for i, (a, bb) in enumerate([(D, D), (D, 512), (512, 256), (256, 1)]):
        din(f"cn_w{i+1}", [a, bb]); din(f"cn_b{i+1}", [bb])
    for i, (a, bb) in enumerate([(D, 512), (512, 256), (256, 1)]):
        din(f"gc_w{i+1}", [a, bb]); din(f"gc_b{i+1}", [bb])
    din("Wk", [D, 256]); din("Mt", [256, 256]); din("Wmem", [256, D])
    din("Wdecay_t", [NCH, NCH, 128, 128])
    din("Wgate_t", [NCH, NCH, 128, 128])
    din("Wc_t", [NCH, 3 * NCH, 128, 128])
    din("Wg_t", [32, D, 128])
    din("Wup_t", [32, D, 128])
    din("Wdown", [4096, D])

    y_d = nc.dram_tensor("y", [TC, D], F32, kind="ExternalOutput").ap()
    cc_in = [nc.dram_tensor(f"cc_in{i}", [128, NCH, 2], F32) for i in range(2)]
    cc_out = [nc.dram_tensor(f"cc_out{i}", [128, NCH, 2], F32) for i in range(2)]

    from contextlib import ExitStack
    es = ExitStack()

    def sbt(name, shape, dt=F32):
        return es.enter_context(nc.sbuf_tensor(name, shape, dt))

    ident = sbt("ident", [128, 128])
    ones_row = sbt("ones_row", [1, 128])
    cnw1 = sbt("cnw1", [128, NCH, D], BF16)
    cnw2 = sbt("cnw2", [128, NCH, 512], BF16)
    cnw3 = sbt("cnw3", [128, 4, 256], BF16)
    cnw4 = sbt("cnw4", [128, 2], BF16)
    gcw1 = sbt("gcw1", [128, NCH, 512], BF16)
    gcw2 = sbt("gcw2", [128, 4, 256], BF16)
    gcw3 = sbt("gcw3", [128, 2], BF16)
    cnb1 = sbt("cnb1", [128, NCH]); cnb2 = sbt("cnb2", [128, 4])
    cnb3 = sbt("cnb3", [128, 2]); cnb4 = sbt("cnb4", [1, 1])
    gcb1 = sbt("gcb1", [128, 4]); gcb2 = sbt("gcb2", [128, 2]); gcb3 = sbt("gcb3", [1, 1])
    bdec = sbt("bdec", [128, NCH]); bgat = sbt("bgat", [128, NCH])
    bc_c = sbt("bc_c", [128, NCH])
    n1w = sbt("n1w", [128, NCH]); n2w = sbt("n2w", [128, NCH])
    convw = sbt("convw", [128, NCH, 7])
    sc_raw = sbt("sc_raw", [1, 8])
    ssm_col = sbt("ssm_col", [128, 1]); ms_col = sbt("ms_col", [128, 1])
    bm_cols = sbt("bm_cols", [128, 2])
    a2_col = [sbt(f"a2_col{i}", [128, 1]) for i in range(2)]
    c2_col = [sbt(f"c2_col{i}", [128, 1]) for i in range(2)]

    NT1 = NBLK * TPB + 1
    mu1 = sbt("mu1", [128, NT1]); iv1 = sbt("iv1", [128, NT1]); nm1 = sbt("nm1", [128, NT1])
    mu2 = sbt("mu2", [128, NT1]); iv2 = sbt("iv2", [128, NT1]); nm2 = sbt("nm2", [128, NT1])
    rr2 = sbt("rr2", [128, NT1]); rr3 = sbt("rr3", [128, NT1])
    vscr = sbt("vscr", [128, NT1]); cscr = sbt("cscr", [128, NT1])
    bn6 = sbt("bn6", [128, 12]); bn2s = sbt("bn2s", [128, 2])
    z1c = sbt("z1c", [128, NT1]); z1pc = sbt("z1pc", [128, NT1])
    ztc = sbt("ztc", [128, NT1]); zsc = sbt("zsc", [128, NT1])
    cpl = sbt("cpl", [128, NT1]); ncpl = sbt("ncpl", [128, NT1])
    accb = sbt("accb", [128, NCH, 2])
    sums = [sbt(f"sums{i}", [128, NCH]) for i in range(2)]
    arstg = sbt("arstg", [128, NCH, 2])
    mcols = sbt("mcols", [128, NCH]); mcols_b = sbt("mcols_b", [128, NCH], BF16)
    gnet = sbt("gnet", [128, 6], BF16)
    gsc = sbt("gsc", [1, 8])
    halo6 = sbt("halo6", [128, NCH, 6], BF16)
    lcd_carry = sbt("lcd_carry", [128, NCH])
    cs2_carry = sbt("cs2_carry", [128, NCH])
    zrow = sbt("zrow", [1, 512])
    bstg = sbt("bstg", [1, 1024])
    twA = sbt("twA", [128, 2, 256]); twB = sbt("twB", [128, 2, 256])
    sqs = sbt("sqs", [128, D])

    xbuf = sbt("xbuf", [128, TPB + 1, D])
    aux = sbt("aux", [128, TPB + 1, D])
    h2T = sbt("h2T", [128, NCH, TB], BF16)

    HT_W = NCH * (TB + 6) // 2
    NRING = 8
    SLOTW = 1536
    SW_NEED = NRING * SLOTW + 1024 + 256
    SCRW = max(9216 + 1024 + HT_W, SW_NEED)
    scr = sbt("scr", [128, SCRW])

    def _reshape(ap, shape):
        if len(shape) == 1:
            return ap
        names = [f"d{i}" for i in range(len(shape))]
        spec = "p (" + " ".join(names) + ") -> p " + " ".join(names)
        kw = {n: s for n, s in zip(names[1:], shape[1:])}
        return ap.rearrange(spec, **kw)

    def scr_f32(off, shape):
        n = int(np.prod(shape))
        assert off + n <= SCRW, (off, shape)
        return _reshape(scr[:, off:off + n], shape)

    def scr_bf16(off, shape):
        n = int(np.prod(shape))
        assert n % 2 == 0 and off + n // 2 <= SCRW, (off, shape)
        return _reshape(scr[:, off:off + n // 2].bitcast(BF16), shape)

    HT_OFF = SCRW - HT_W
    hT = scr_bf16(HT_OFF, [NCH, TB + 6])
    outT = scr_bf16(0, [NCH, 512])
    g1b = scr_bf16(2048, [NCH, 512])
    g2b = scr_bf16(4096, [4, 512])
    g3b = scr_bf16(5120, [2, 512])
    wdg_h = scr_bf16(0, [2, 4, NCH, 128])
    st_scr = [scr_f32(4096 + i * 512, [512]) for i in range(4)]
    memT_b = scr_bf16(0, [NCH, TB])
    kT_s = scr_bf16(9728, [2, 512])
    moT_s = scr_bf16(6144, [2, 512])
    wk_sl = scr_bf16(7424, [NCH, 256])
    mt_sl = scr_bf16(8448, [2, 256])
    wme_sl = scr_bf16(8704, [2, D])
    cv_scr = [scr_f32(4096 + i * 512, [512]) for i in range(2)]
    wc_ring = [scr_bf16(5120 + i * 1536, [3 * NCH, 128]) for i in range(2)]
    combT = [scr_f32(8192 + i * 512, [512]) for i in range(2)]
    assert 10240 <= HT_OFF
    swg_wg = [scr_bf16(i * SLOTW, [NCH, 128]) for i in range(NRING)]
    swg_wu = [scr_bf16(i * SLOTW + 512, [NCH, 128]) for i in range(NRING)]
    swg_wd = [scr_bf16(i * SLOTW + 1024, [NCH, 128]) for i in range(NRING)]
    SW_OFF = NRING * SLOTW
    prod_buf = scr_bf16(SW_OFF, [NCH, 256])
    silu_sb = [scr_bf16(SW_OFF + 1024 + i * 128, [256]) for i in range(2)]

    P = [es.enter_context(nc.psum_tensor(f"P{i}", [128, 512], F32)) for i in range(4)]
    PF = es.enter_context(nc.psum_tensor("PF", [128, 2048], F32))

    pl = Plan(nc)
    for s in ("pe", "dv", "ac", "gp", "xld", "xst", "winit", "binit", "ccs", "ccc", "mxs", "yac"):
        pl.sem(s)
    swg_sem = [f"sg{i}" for i in range(NRING)]
    wc_sem = ["wc0", "wc1"]
    for s in swg_sem + wc_sem:
        pl.sem(s)

    p_free = [None, None, None, None]
    pf_free = None

    def mm(bank_ap, lhsT_ap, rhs_ap, start, stop, inc=True):
        def f(t, o=bank_ap, a=lhsT_ap, r=rhs_ap, st=start, sp=stop):
            return t.matmul(o, a, r, start=st, stop=sp)
        return pl.emit(ET, f, inc="pe" if inc else None)

    def transp(bank_ap, in_ap, inc=True, np_=128, start=True, stop=True):
        def f(t, o=bank_ap, i=in_ap, n=np_, st=start, sp=stop):
            return t.matmul(o, i, ident[0:n, 0:n], start=st, stop=sp,
                            is_transpose=True)
        return pl.emit(ET, f, inc="pe" if inc else None)

    def stats_tile(src, col, mu_t, npart=128):
        pl.emit(ES, lambda s, x=src, c=col, n=npart: s.activation(
            sqs[0:n, :], x[0:n, :], ACT.Square,
            accum_out=vscr[0:n, c:c + 1]), inc="ac")
        return pl.emit(EV, lambda v, x=src, c=col, n=npart: v.reduce_sum(
            mu_t[0:n, c:c + 1], x[0:n, :], axis=AX.X), inc="dv")

    def stat_cols(mu_t, iv_t, nm_t, c0, c1, eps, r_t=None, r_only=False):
        V = vscr[:, c0:c1]; MU = mu_t[:, c0:c1]
        T0 = cscr[:, c0:c1]; A = zsc[:, c0:c1]
        pl.wait(EV, pl.mark("ac"))

        def rsqrt_into(dst, scratch):
            m = pl.mark("dv")
            pl.wait(ES, m)
            pl.emit(ES, lambda s: s.activation(dst, T0, ACT.Sqrt), inc="ac")
            pl.wait(EV, pl.mark("ac"))
            pl.emit(EV, lambda v: v.reciprocal(dst, dst))
            pl.emit(EV, lambda v: v.tensor_mul(scratch, dst, dst))
            pl.emit(EV, lambda v: v.tensor_mul(scratch, scratch, T0))
            pl.emit(EV, lambda v: v.tensor_scalar(
                scratch, scratch, -0.5, 1.5, ALU.mult, ALU.add))
            pl.emit(EV, lambda v: v.tensor_mul(dst, dst, scratch))

        if r_only:
            R = r_t[:, c0:c1]
            pl.emit(EV, lambda v: v.tensor_scalar(
                T0, V, float(1.0 / D), float(RMS_EPS), ALU.mult, ALU.add),
                inc="dv")
            rsqrt_into(R, A)
            return pl.emit(EV, lambda v: v.engine_nop(), inc="dv")

        IV = iv_t[:, c0:c1]; NM = nm_t[:, c0:c1]
        pl.emit(EV, lambda v: v.tensor_scalar_mul(MU, MU, float(1.0 / D)))
        pl.emit(EV, lambda v: v.tensor_scalar_mul(A, V, float(1.0 / D)))
        pl.emit(EV, lambda v: v.tensor_mul(V, MU, MU))
        pl.emit(EV, lambda v: v.tensor_sub(A, A, V))
        pl.emit(EV, lambda v: v.tensor_scalar_add(T0, A, float(eps)), inc="dv")
        rsqrt_into(IV, V)
        pl.emit(EV, lambda v: v.scalar_tensor_tensor(NM, MU, -1.0, IV,
                                                     ALU.mult, ALU.mult))
        mret = pl.emit(EV, lambda v: v.engine_nop(), inc="dv")
        if r_t is not None:
            R = r_t[:, c0:c1]
            pl.emit(EV, lambda v: v.tensor_mul(V, IV, IV))
            pl.emit(EV, lambda v: v.tensor_mul(V, V, A))
            pl.emit(EV, lambda v: v.tensor_scalar_add(T0, V, float(RMS_EPS)),
                    inc="dv")
            rsqrt_into(R, V)
            mret = pl.emit(EV, lambda v: v.engine_nop(), inc="dv")
        return mret

    pl.emit(EG, lambda g: g.memset(ident[:, :], 0.0), inc="gp")
    pl.wait(EG, pl.mark("gp"))
    pl.emit(EG, lambda g: g.affine_select(
        out=ident[:, :], in_=ident[:, :], compare_op=ALU.not_equal,
        fill=1.0, base=0, pattern=[[-1, 128]], channel_multiplier=1), inc="gp")
    pl.emit(EG, lambda g: g.memset(ones_row[:, :], 1.0), inc="gp")
    for bname in (sums[0], sums[1], lcd_carry, cs2_carry, mu1, mu2, vscr,
                  cscr, zsc, z1c, z1pc, ztc, cpl, ncpl, iv1, nm1, iv2, nm2,
                  rr2, rr3, zrow, gsc, bstg):
        pl.emit(EG, lambda g, t=bname: g.memset(t[:, :], 0.0), inc="gp")
    pl.emit(EG, lambda g: g.memset(halo6[:, :, :], 0.0), inc="gp")
    setup_done = pl.mark("gp")

    def ldw(dst, src):
        pl.emit(EG, lambda g, d=dst, s=src: g.dma_start(out=d, in_=s),
                inc="winit", inck=16)

    ldw(cnw1[:, :, :], dr["cn_w1"].rearrange("(c p) m -> p c m", p=128))
    ldw(cnw2[:, :, :], dr["cn_w2"].rearrange("(c p) m -> p c m", p=128))
    ldw(cnw3[:, :, :], dr["cn_w3"].rearrange("(c p) m -> p c m", p=128))
    ldw(gcw1[:, :, :], dr["gc_w1"].rearrange("(c p) m -> p c m", p=128))
    ldw(gcw2[:, :, :], dr["gc_w2"].rearrange("(c p) m -> p c m", p=128))
    ldw(cnb4[:, :], dr["cn_b4"][None, :])
    ldw(gcb3[:, :], dr["gc_b3"][None, :])
    ldw(convw[:, :, :], dr["conv_w"].rearrange("(c p) k -> p c k", p=128))
    ldw(sc_raw[0:1, 0:1], dr["state_scale"][None, :])
    ldw(sc_raw[0:1, 1:2], dr["mem_scale"][None, :])
    ldw(sc_raw[0:1, 2:3], dr["ar"][None, :])
    ldw(sc_raw[0:1, 3:4], dr["smask"][None, :])
    ldw(sc_raw[0:1, 4:6], dr["bm"][None, :])
    winit_done = pl.mark("winit")

    col_specs = [
        ("cn_b1", cnb1, NCH), ("cn_b2", cnb2, 4), ("cn_b3", cnb3, 2),
        ("gc_b1", gcb1, 4), ("gc_b2", gcb2, 2),
        ("bdecay", bdec, NCH), ("bgate", bgat, NCH), ("bc", bc_c, NCH),
        ("norm1_w", n1w, NCH), ("norm2_w", n2w, NCH),
        ("cn_w4", cnw4, 2), ("gc_w3", gcw3, 2),
    ]
    pl.waits(ET, [setup_done])
    nvec = 0
    for (name, dst, ncol) in col_specs:
        nvec += 1
        pl.emit(EG, lambda g, name=name, n=ncol: g.dma_start(
            out=bstg[0:1, 0:n * 128], in_=dr[name].rearrange("(a) -> 1 a")
            if False else dr[name][None, :]), inc="binit", inck=16)
        m_row = ("binit", nvec * 16)
        bank = 3
        pl.waits(ET, [m_row, p_free[bank]])
        for c in range(ncol):
            transp(P[bank][:, c:c + 1], bstg[0:1, c * 128:(c + 1) * 128],
                   inc=(c == ncol - 1), np_=1,
                   start=(c == 0), stop=(c == ncol - 1))
        pl.wait(ES, pl.mark("pe"))
        pl.emit(ES, lambda s, d=dst, n=ncol, b=bank: s.copy(
            d[:, 0:n], P[b][:, 0:n]), inc="ac")
        m_bst = pl.mark("ac")
        p_free[bank] = m_bst
        pl.wait(EG, m_bst)
    bcols_done = pl.mark("ac")

    pl.wait(EV, winit_done)
    pl.wait(EV, bcols_done)
    m_ssm = pl.emit(EV, lambda v: v.tensor_mul(
        sc_raw[0:1, 6:7], sc_raw[0:1, 0:1], sc_raw[0:1, 3:4]), inc="dv")
    pl.waits(ET, [setup_done, m_ssm])
    mm(P[0][:, 0:1], ones_row[:, :], sc_raw[0:1, 6:7], True, False, inc=False)
    mm(P[0][:, 1:2], ones_row[:, :], sc_raw[0:1, 1:2], False, False, inc=False)
    mm(P[0][:, 2:3], ones_row[:, :], sc_raw[0:1, 4:5], False, False, inc=False)
    mm(P[0][:, 3:4], ones_row[:, :], sc_raw[0:1, 5:6], False, True, inc=True)
    pl.wait(ES, pl.mark("pe"))
    pl.emit(ES, lambda s: s.copy(ssm_col[:, :], P[0][:, 0:1]), inc="ac")
    pl.emit(ES, lambda s: s.copy(ms_col[:, :], P[0][:, 1:2]), inc="ac")
    pl.emit(ES, lambda s: s.copy(bm_cols[:, :], P[0][:, 2:4]), inc="ac")
    p_free[0] = pl.mark("ac")

    zrow_free = None

    def chain_slice(src_fn, width=512):
        nonlocal zrow_free
        specs = [(NCH, NCH, cnw1, cnb1, None, g1b),
                 (4, NCH, cnw2, cnb2, g1b, g2b),
                 (2, 4, cnw3, cnb3, g2b, g3b)]
        pl.wait(ET, pl.mark("ac"))
        for (no, nk, wt, bt, src_g, dst_g) in specs:
            if src_g is not None:
                pl.wait(ET, pl.mark("ac"))
            for o in range(no):
                bank = o % 2
                pl.wait(ET, p_free[bank])
                for k in range(nk):
                    rhs = src_fn(k) if src_g is None else src_g[:, k, 0:width]
                    mm(P[bank][:, 0:width], wt[:, k, o * 128:(o + 1) * 128],
                       rhs, k == 0, k == nk - 1, inc=(k == nk - 1))
                pl.wait(ES, pl.mark("pe"))
                pl.emit(ES, lambda s, o=o, b=bank, w=width, dg=dst_g, bt=bt:
                        s.activation(dg[:, o, 0:w], P[b][:, 0:w], ACT.Gelu,
                                     bias=bt[:, o:o + 1]), inc="ac")
                p_free[bank] = pl.mark("ac")
        bank = 2
        pl.waits(ET, [p_free[bank], pl.mark("ac")])
        for k in range(2):
            mm(P[bank][0:1, 0:width], cnw4[:, k:k + 1], g3b[:, k, 0:width],
               k == 0, k == 1, inc=(k == 1))
        pl.waits(ES, [pl.mark("pe"), zrow_free])
        mz = pl.emit(ES, lambda s, w=width: s.activation(
            zrow[0:1, 0:w], P[2][0:1, 0:w], ACT.Identity, bias=cnb4[0:1, 0:1]),
            inc="ac")
        p_free[bank] = pl.mark("ac")
        return mz

    def zrow_to_cols(dst, col0, nq, m_z, last_np=128):
        nonlocal zrow_free
        pl.wait(ET, m_z)
        bank = 3
        pl.wait(ET, p_free[bank])
        for q in range(nq):
            npq = last_np if q == nq - 1 else 128
            transp(P[bank][0:npq, q * 128:q * 128 + 1],
                   zrow[0:1, q * 128:q * 128 + npq], inc=(q == nq - 1), np_=1,
                   start=(q == 0), stop=(q == nq - 1))
        zrow_free = pl.mark("pe")
        pl.wait(ES, zrow_free)
        def f(s, d=dst, c=col0, n=nq, b=bank):
            if n > 1:
                src = P[b][:, 0:n * 128].rearrange("p (q o) -> p q o", o=128)[:, :, 0]
            else:
                src = P[b][:, 0:1]
            return s.copy(d[:, c:c + n], src)
        mc = pl.emit(ES, f, inc="ac")
        p_free[bank] = pl.mark("ac")
        return mc

    twist_prev = [None]

    def coupling_from_z(zsrc, c0, c1, idx, m_ready):
        pl.waits(ES, [m_ready, twist_prev[0]])
        pl.emit(ES, lambda s: s.activation(
            cpl[:, c0:c1], zsrc[:, c0:c1], ACT.Tanh, scale=0.5), inc="ac")
        m = pl.mark("ac")
        pl.wait(EV, m)
        pl.emit(EV, lambda v: v.tensor_scalar(
            cpl[:, c0:c1], cpl[:, c0:c1], c2_col[idx][:, 0:1],
            a2_col[idx][:, 0:1], ALU.mult, ALU.add))
        pl.emit(EV, lambda v: v.tensor_scalar_mul(ncpl[:, c0:c1], cpl[:, c0:c1], -1.0))
        return pl.emit(EV, lambda v: v.engine_nop(), inc="dv")

    twist_slot_free = [None, None]

    def twist_tile(buf, tile, npart, ccol, m_cpl, slot=0):
        t = tile
        A = buf[0:npart, t, 0:256]; Bq = buf[0:npart, t, 256:512]
        C = buf[0:npart, t, 512:768]; Dq = buf[0:npart, t, 768:1024]
        cpc = cpl[0:npart, ccol:ccol + 1]; ncc = ncpl[0:npart, ccol:ccol + 1]
        tA = twA[0:npart, slot, :]; tB = twB[0:npart, slot, :]
        pl.waits(EV, [m_cpl, twist_slot_free[slot]])
        pl.emit(EV, lambda v: v.tensor_copy(tA, A), wide=_w('twist'))
        pl.emit(EV, lambda v: v.tensor_copy(tB, Bq), wide=_w('twist'))
        pl.emit(EV, lambda v: v.scalar_tensor_tensor(A, C, cpc, A, ALU.mult, ALU.add), wide=_w('twist'))
        pl.emit(EV, lambda v: v.scalar_tensor_tensor(C, tA, ncc, C, ALU.mult, ALU.add), wide=_w('twist'))
        pl.emit(EV, lambda v: v.scalar_tensor_tensor(Bq, Dq, ncc, Bq, ALU.mult, ALU.add), wide=_w('twist'))
        m4 = pl.emit(EV, lambda v: v.scalar_tensor_tensor(Dq, tB, cpc, Dq, ALU.mult, ALU.add), inc="dvo", wide=_w('twist'))
        pl.wait(EV, m4)
        mfin = pl.emit(EV, lambda v: v.engine_nop(), inc="dv")
        twist_prev[0] = mfin
        twist_slot_free[slot] = mfin
        return mfin

    xbuf_free = None
    aux_free = None
    ht_free = None
    m_sums1 = None
    accb_free = None

    for b in range(NBLK):
        halo = (b == 0)
        col0 = b * TPB
        pl.wait(EY, xbuf_free)
        pl.emit(EY, lambda y, b=b: y.dma_start(
            out=xbuf[:, 0:TPB, :],
            in_=x_d[b * TB:(b + 1) * TB, :].rearrange("(k p) d -> p k d", p=128)),
            inc="xld", inck=16)
        if halo:
            pl.emit(EY, lambda y: y.dma_start(
                out=xbuf[0:6, TPB, :], in_=xh_d), inc="xld", inck=16)
        m_xld = pl.mark("xld")

        tl = [(t, col0 + t, 128) for t in range(TPB)]
        if halo:
            tl.append((TPB, HCOL, 6))
        pl.wait(EV, m_xld)
        pl.wait(ES, m_xld)
        for (t, col, npart) in tl:
            stats_tile(xbuf[:, t, :], col, mu1, npart)
        if halo:
            stat_cols(mu1, iv1, nm1, col0, col0 + TPB, LN_EPS)
            m_stats = stat_cols(mu1, iv1, nm1, HCOL, HCOL + 1, LN_EPS)
        else:
            m_stats = stat_cols(mu1, iv1, nm1, col0, col0 + TPB, LN_EPS)

        pl.waits(ES, [m_stats, aux_free])
        for (t, col, npart) in tl:
            pl.emit(ES, lambda s, t=t, c=col, n=npart: s.activation(
                aux[0:n, t, :], xbuf[0:n, t, :], ACT.Identity,
                scale=iv1[0:n, c:c + 1], bias=nm1[0:n, c:c + 1]), inc="ac")
        m_lnx = pl.mark("ac")
        xbuf_free = m_lnx

        pl.wait(ET, m_lnx)
        for s_ in range(NSL):
            for c in range(NCH):
                bank = (s_ * NCH + c) % 2
                pl.waits(ET, [p_free[bank], ht_free])
                for q in range(4):
                    transp(P[bank][:, q * 128:(q + 1) * 128],
                           aux[:, s_ * 4 + q, c * 128:(c + 1) * 128], inc=(q == 3),
                           start=(q == 0), stop=(q == 3))
                pl.waits(ES, [pl.mark("pe"), accb_free])
                pl.emit(ES, lambda s, c=c, s_=s_, b2=bank: s.activation(
                    hT[:, c, 6 + s_ * 512: 6 + (s_ + 1) * 512], P[b2][:, 0:512],
                    ACT.Identity, accum_out=accb[:, c, s_:s_ + 1]), inc="ac")
                p_free[bank] = pl.mark("ac")
        m_evac = pl.mark("ac")

        pl.wait(EV, m_evac)
        pl.emit(EV, lambda v: v.tensor_reduce(
            accb[:, :, 0:1], accb[:, :, :], AX.X, ALU.add))
        m_sums1 = pl.emit(EV, lambda v: v.tensor_add(
            sums[0][:, :], sums[0][:, :], accb[:, :, 0]), inc="dv")
        accb_free = m_sums1

        for s_ in range(NSL):
            mz = chain_slice(lambda k, s_=s_: hT[:, k, 6 + s_ * 512:6 + (s_ + 1) * 512])
            zrow_to_cols(z1c, col0 + s_ * 4, 4, mz)
        ht_free = pl.mark("pe")
        aux_free = ht_free

    pl.waits(ET, [ht_free])
    bank = 2
    pl.wait(ET, p_free[bank])
    for c in range(NCH):
        transp(P[bank][:, c * 64:c * 64 + 6],
               aux[0:6, TPB, c * 128:(c + 1) * 128], inc=(c == NCH - 1), np_=6,
               start=(c == 0), stop=(c == NCH - 1))
    pl.wait(ES, pl.mark("pe"))
    def f_he(s, b2=bank):
        src = P[b2][:, 0:NCH * 64].rearrange("p (c o) -> p c o", o=64)
        return s.copy(hT[:, :, 0:6], src[:, :, 0:6])
    pl.emit(ES, f_he, inc="ac")
    p_free[bank] = pl.mark("ac")
    mzh = chain_slice(lambda k: hT[:, k, 0:6], width=6)
    zrow_to_cols(z1c, HCOL, 1, mzh, last_np=6)
    ht_free = pl.mark("pe")
    aux_free = ht_free

    def allreduce_gf(idx, m_ready):
        ci, co = cc_in[idx], cc_out[idx]
        pl.wait(EV, m_ready)
        pl.emit(EV, lambda v: v.tensor_scalar_mul(
            arstg[:, :, 0], sums[idx][:, :], bm_cols[:, 0:1]))
        m_stg = pl.emit(EV, lambda v: v.tensor_scalar_mul(
            arstg[:, :, 1], sums[idx][:, :], bm_cols[:, 1:2]), inc="dv")
        pl.wait(EG, m_stg)
        pl.emit(EG, lambda g: g.dma_start(
            out=ci.ap(), in_=arstg[:, :, :]), inc="ccs", inck=16)
        pl.wait(EG, pl.mark("ccs"))
        pl.emit(EG, lambda g: g.collective_compute(
            "AllReduce", ALU.add, replica_groups=[list(range(NCORE))],
            ins=[ci.ap().opt()], outs=[co.ap().opt()]), inc="ccc", inck=1)
        pl.wait(EG, pl.mark("ccc"))
        pl.emit(EG, lambda g: g.dma_start(
            out=arstg[:, :, :], in_=co.ap()), inc="ccs", inck=16)
        m_ld = pl.mark("ccs")
        pl.wait(EV, m_ld)
        pl.emit(EV, lambda v: v.tensor_scalar_mul(
            arstg[:, :, 0], arstg[:, :, 0], bm_cols[:, 0:1]))
        pl.emit(EV, lambda v: v.tensor_scalar_mul(
            arstg[:, :, 1], arstg[:, :, 1], bm_cols[:, 1:2]))
        pl.emit(EV, lambda v: v.tensor_add(mcols[:, :], arstg[:, :, 0], arstg[:, :, 1]))
        pl.emit(EV, lambda v: v.tensor_scalar_mul(
            mcols[:, :], mcols[:, :], float(1.0 / seq_total)))
        m_m = pl.emit(EV, lambda v: v.tensor_copy(mcols_b[:, :], mcols[:, :]),
                      inc="dv")
        pl.wait(ET, m_m)
        bank = 0
        pl.wait(ET, p_free[bank])
        for o in range(4):
            for k in range(NCH):
                mm(P[bank][:, o:o + 1], gcw1[:, k, o * 128:(o + 1) * 128],
                   mcols_b[:, k:k + 1], (o == 0 and k == 0),
                   (o == 3 and k == NCH - 1), inc=(o == 3 and k == NCH - 1))
        pl.wait(ES, pl.mark("pe"))
        for o in range(4):
            pl.emit(ES, lambda s, o=o, b=bank: s.activation(
                gnet[:, o:o + 1], P[b][:, o:o + 1], ACT.Gelu,
                bias=gcb1[:, o:o + 1]), inc="ac")
        m_g1 = pl.mark("ac")
        p_free[bank] = m_g1
        bank = 1
        pl.waits(ET, [m_g1, p_free[bank]])
        for o in range(2):
            for k in range(4):
                mm(P[bank][:, o:o + 1], gcw2[:, k, o * 128:(o + 1) * 128],
                   gnet[:, k:k + 1], (o == 0 and k == 0),
                   (o == 1 and k == 3), inc=(o == 1 and k == 3))
        pl.wait(ES, pl.mark("pe"))
        for o in range(2):
            pl.emit(ES, lambda s, o=o, b=bank: s.activation(
                gnet[:, 4 + o:5 + o], P[b][:, o:o + 1], ACT.Gelu,
                bias=gcb2[:, o:o + 1]), inc="ac")
        m_g2 = pl.mark("ac")
        p_free[bank] = m_g2
        bank = 2
        pl.waits(ET, [m_g2, p_free[bank]])
        for k in range(2):
            mm(P[bank][0:1, 0:1], gcw3[:, k:k + 1], gnet[:, 4 + k:5 + k],
               k == 0, k == 1, inc=(k == 1))
        pl.wait(ES, pl.mark("pe"))
        pl.emit(ES, lambda s, b=bank: s.copy(gsc[0:1, 0:1], P[b][0:1, 0:1]), inc="ac")
        m_z3 = pl.mark("ac")
        p_free[bank] = m_z3
        pl.wait(EV, m_z3)
        V = gsc
        m_zg = pl.emit(EV, lambda v: v.tensor_add(
            V[0:1, 0:1], V[0:1, 0:1], gcb3[0:1, 0:1]), inc="dv")
        pl.wait(ES, m_zg)
        pl.emit(ES, lambda s: s.activation(V[0:1, 1:2], V[0:1, 0:1], ACT.Tanh,
                                           scale=0.5), inc="ac")
        m_th = pl.mark("ac")
        pl.wait(EV, m_th)
        pl.emit(EV, lambda v: v.tensor_scalar(
            V[0:1, 1:2], V[0:1, 1:2], 0.5, 0.5, ALU.mult, ALU.add))
        pl.emit(EV, lambda v: v.tensor_scalar(
            V[0:1, 2:3], V[0:1, 1:2], 0.7, -0.5, ALU.mult, ALU.add))
        pl.emit(EV, lambda v: v.tensor_mul(V[0:1, 2:3], V[0:1, 2:3], sc_raw[0:1, 2:3]))
        pl.emit(EV, lambda v: v.tensor_scalar(
            V[0:1, 2:3], V[0:1, 2:3], 2.0, 0.1, ALU.mult, ALU.add))
        pl.emit(EV, lambda v: v.tensor_scalar_mul(V[0:1, 3:4], sc_raw[0:1, 2:3], 0.6))
        pl.emit(EV, lambda v: v.tensor_scalar_mul(V[0:1, 4:5], V[0:1, 3:4], 0.5))
        pl.emit(EV, lambda v: v.tensor_add(V[0:1, 5:6], V[0:1, 2:3], V[0:1, 4:5]))
        m_sc = pl.emit(EV, lambda v: v.engine_nop(), inc="dv")
        bank = 3
        pl.waits(ET, [m_sc, p_free[bank]])
        mm(P[bank][:, 0:1], ones_row[:, :], V[0:1, 5:6], True, False, inc=False)
        mm(P[bank][:, 1:2], ones_row[:, :], V[0:1, 4:5], False, True, inc=True)
        pl.wait(ES, pl.mark("pe"))
        pl.emit(ES, lambda s: s.copy(a2_col[idx][:, :], P[bank][:, 0:1]), inc="ac")
        pl.emit(ES, lambda s: s.copy(c2_col[idx][:, :], P[bank][:, 1:2]), inc="ac")
        m_done = pl.mark("ac")
        p_free[bank] = m_done
        return m_done

    pl.wait(EY, xbuf_free)
    pl.emit(EY, lambda y: y.dma_start(
        out=xbuf[:, 0:TPB, :],
        in_=x_d[0:TB, :].rearrange("(k p) d -> p k d", p=128)),
        inc="xld", inck=16)
    pl.emit(EY, lambda y: y.dma_start(
        out=xbuf[0:6, TPB, :], in_=xh_d), inc="xld", inck=16)
    m_xld0 = pl.mark("xld")
    tl0 = [(t, t, 128) for t in range(TPB)] + [(TPB, HCOL, 6)]
    pl.waits(ES, [m_xld0, aux_free])
    for (t, col, npart) in tl0:
        pl.emit(ES, lambda s, t=t, c=col, n=npart: s.activation(
            aux[0:n, t, :], xbuf[0:n, t, :], ACT.Identity,
            scale=iv1[0:n, c:c + 1], bias=nm1[0:n, c:c + 1]), inc="ac")
    pre2 = (m_xld0, pl.mark("ac"))

    m_gfA = allreduce_gf(0, m_sums1)

    m_sums2 = None
    swg_slot_free = [None] * NRING
    swg_slot_count = [0] * NRING
    wc_slot_free = [None, None]
    wc_slot_count = [0, 0]
    mxs_count = 0
    halo6_ready = setup_done
    scr_mixer_gate = None
    x2st_marks = []

    def ffn_cycles(zc_first, idx_cols, blk, with_halo, m_apply, buf=None):
        if buf is None:
            buf = aux
        col0 = blk * TPB
        m_cp = coupling_from_z(zc_first, col0, col0 + TPB, idx_cols, m_apply)
        m_slice = [None] * NSL
        m_halo = None
        for s_ in range(NSL):
            for q in range(4):
                m_slice[s_] = twist_tile(buf, s_ * 4 + q, 128,
                                         col0 + s_ * 4 + q, m_cp, slot=s_ % 2)
        if with_halo:
            m_cph = coupling_from_z(zc_first, HCOL, HCOL + 1, idx_cols, m_apply)
            m_halo = twist_tile(buf, TPB, 6, HCOL, m_cph)
        for cyc in (2, 3):
            for s_ in range(NSL):
                pl.wait(ET, m_slice[s_])
                for c in range(NCH):
                    bank = c % 2
                    pl.wait(ET, p_free[bank])
                    for q in range(4):
                        transp(P[bank][:, q * 128:(q + 1) * 128],
                               buf[:, s_ * 4 + q, c * 128:(c + 1) * 128],
                               inc=(q == 3), start=(q == 0), stop=(q == 3))
                    pl.wait(ES, pl.mark("pe"))
                    pl.emit(ES, lambda s, c=c, b2=bank: s.activation(
                        outT[:, c, 0:512], P[b2][:, 0:512], ACT.Identity), inc="ac")
                    p_free[bank] = pl.mark("ac")
                m_tr = pl.mark("ac")
                pl.wait(ET, m_tr)
                mz = chain_slice(lambda k: outT[:, k, 0:512])
                mzc = zrow_to_cols(ztc, col0 + s_ * 4, 4, mz)
                m_cp2 = coupling_from_z(ztc, col0 + s_ * 4, col0 + s_ * 4 + 4,
                                        idx_cols, mzc)
                for q in range(4):
                    m_slice[s_] = twist_tile(buf, s_ * 4 + q, 128,
                                             col0 + s_ * 4 + q, m_cp2,
                                             slot=s_ % 2)
            if with_halo:
                pl.wait(ET, m_halo)
                bank = 2
                pl.wait(ET, p_free[bank])
                for c in range(NCH):
                    transp(P[bank][:, c * 64:c * 64 + 6],
                           buf[0:6, TPB, c * 128:(c + 1) * 128], inc=(c == NCH - 1), np_=6,
                           start=(c == 0), stop=(c == NCH - 1))
                pl.wait(ES, pl.mark("pe"))
                def fh(s, b2=bank):
                    src = P[b2][:, 0:NCH * 64].rearrange("p (c o) -> p c o", o=64)
                    return s.copy(outT[:, :, 0:6], src[:, :, 0:6])
                pl.emit(ES, fh, inc="ac")
                p_free[bank] = pl.mark("ac")
                mzh2 = chain_slice(lambda k: outT[:, k, 0:6], width=6)
                mzch = zrow_to_cols(ztc, HCOL, 1, mzh2, last_np=6)
                m_cp3 = coupling_from_z(ztc, HCOL, HCOL + 1, idx_cols, mzch)
                m_halo = twist_tile(buf, TPB, 6, HCOL, m_cp3)
        m_all = [m for m in m_slice + [m_halo] if m]
        last = max(m_all, key=lambda m: m[1])
        return pl.mark("pe"), last, m_slice, m_halo

    for b in range(NBLK):
        halo = (b == 0)
        col0 = b * TPB
        in_state = (b < SBLK)
        tl = [(t, col0 + t, 128) for t in range(TPB)]
        if halo:
            tl.append((TPB, HCOL, 6))

        if b == 0:
            m_xld, m_apply = pre2
        else:
            pl.wait(EY, xbuf_free)
            pl.emit(EY, lambda y, b=b: y.dma_start(
                out=xbuf[:, 0:TPB, :],
                in_=x_d[b * TB:(b + 1) * TB, :].rearrange("(k p) d -> p k d", p=128)),
                inc="xld", inck=16)
            m_xld = pl.mark("xld")
            pl.waits(ES, [m_xld, aux_free])
            for (t, col, npart) in tl:
                pl.emit(ES, lambda s, t=t, c=col, n=npart: s.activation(
                    aux[0:n, t, :], xbuf[0:n, t, :], ACT.Identity,
                    scale=iv1[0:n, c:c + 1], bias=nm1[0:n, c:c + 1]), inc="ac")
            m_apply = pl.mark("ac")

        if stop_after == "lnx":
            if b == 0:
                dbg = nc.dram_tensor("dbg", [128, 5, NT1], F32,
                                     kind="ExternalOutput").ap()
                pl.waits(EY, [m_apply])
                for i, tsr in enumerate((mu1, iv1, nm1, vscr, cscr)):
                    pl.emit(EY, lambda y, i=i, t=tsr: y.dma_start(
                        out=dbg[:, i, :], in_=t[:, :]), inc="xst", inck=16)
            pl.waits(EY, [m_apply])
            m_yst = pl.emit(EY, lambda y, b=b: y.dma_start(
                out=y_d[b * TB:(b + 1) * TB, :].rearrange("(k p) d -> p k d", p=128),
                in_=aux[:, 0:TPB, :]), inc="xst", inck=16)
            xbuf_free = m_apply
            aux_free = m_yst
            scr_mixer_gate = pl.mark("pe")
            continue

        if stop_after == "c1":
            m_cp = coupling_from_z(z1c, col0, col0 + TPB, 0, m_apply)
            m_last = None
            for t in range(TPB):
                m_last = twist_tile(aux, t, 128, col0 + t, m_cp)
            pl.waits(EY, [m_last])
            m_yst = pl.emit(EY, lambda y, b=b: y.dma_start(
                out=y_d[b * TB:(b + 1) * TB, :].rearrange("(k p) d -> p k d", p=128),
                in_=aux[:, 0:TPB, :]), inc="xst", inck=16)
            xbuf_free = m_apply
            aux_free = m_yst
            scr_mixer_gate = pl.mark("pe")
            continue

        m_chain2, m_aux_upd, m_sl_p2, m_halo_p2 = ffn_cycles(
            z1c, 0, b, halo, m_apply)

        if stop_after == "x1":
            pl.wait(EV, m_aux_upd)
            m_x1 = pl.emit(EV, lambda v: v.scalar_tensor_tensor(
                xbuf[:, 0:TPB, :], aux[:, 0:TPB, :], 0.5, xbuf[:, 0:TPB, :],
                ALU.mult, ALU.add), inc="dv", wide=_w('scale'))
            pl.waits(EY, [m_x1])
            m_yst = pl.emit(EY, lambda y, b=b: y.dma_start(
                out=y_d[b * TB:(b + 1) * TB, :].rearrange("(k p) d -> p k d", p=128),
                in_=xbuf[:, 0:TPB, :]), inc="xst", inck=16)
            xbuf_free = m_yst
            aux_free = m_x1
            scr_mixer_gate = m_chain2
            continue

        m_h_sl = [None] * NSL
        for s_ in range(NSL):
            tls = [(s_ * 4 + q, col0 + s_ * 4 + q, 128) for q in range(4)]
            pl.wait(EV, m_sl_p2[s_])
            m_x1s = pl.emit(EV, lambda v, s_=s_: v.scalar_tensor_tensor(
                xbuf[:, s_ * 4:(s_ + 1) * 4, :], aux[:, s_ * 4:(s_ + 1) * 4, :],
                0.5, xbuf[:, s_ * 4:(s_ + 1) * 4, :], ALU.mult, ALU.add),
                inc="dv", wide=_w('scale'))
            if halo and s_ == NSL - 1:
                tls.append((TPB, HCOL, 6))
                pl.wait(EV, m_halo_p2)
                m_x1s = pl.emit(EV, lambda v: v.scalar_tensor_tensor(
                    xbuf[0:6, TPB, :], aux[0:6, TPB, :], 0.5, xbuf[0:6, TPB, :],
                    ALU.mult, ALU.add), inc="dv", wide=_w('scale'))
            pl.wait(ES, m_x1s)
            for (t, col, npart) in tls:
                stats_tile(xbuf[:, t, :], col, mu2, npart)
            if halo and s_ == NSL - 1:
                stat_cols(mu2, iv2, nm2, col0 + s_ * 4, col0 + s_ * 4 + 4,
                          LN_EPS, r_t=rr2)
                m_st2 = stat_cols(mu2, iv2, nm2, HCOL, HCOL + 1, LN_EPS,
                                  r_t=rr2)
            else:
                m_st2 = stat_cols(mu2, iv2, nm2, col0 + s_ * 4,
                                  col0 + s_ * 4 + 4, LN_EPS, r_t=rr2)
            pl.wait(ES, m_st2)
            for (t, col, npart) in tls:
                pl.emit(ES, lambda s, t=t, c=col, n=npart: s.activation(
                    aux[0:n, t, :], xbuf[0:n, t, :], ACT.Identity,
                    scale=iv2[0:n, c:c + 1], bias=nm2[0:n, c:c + 1]), inc="ac")
            m_lnx1s = pl.mark("ac")
            pl.wait(EV, m_lnx1s)
            for (t, col, npart) in tls:
                pl.emit(EV, lambda v, t=t, c=col, n=npart: v.tensor_scalar_mul(
                    aux[0:n, t, :], aux[0:n, t, :], rr2[0:n, c:c + 1]), inc="dv",
                    wide=_w('scale'))
            m_h_sl[s_] = pl.mark("dv")
        m_x1 = m_h_sl[NSL - 1]

        pl.wait(ET, ht_free)
        for s_ in range(NSL):
            pl.wait(ET, m_h_sl[s_])
            for c in range(NCH):
                bank = (s_ * NCH + c) % 2
                pl.wait(ET, p_free[bank])
                for q in range(4):
                    transp(P[bank][:, q * 128:(q + 1) * 128],
                           aux[:, s_ * 4 + q, c * 128:(c + 1) * 128], inc=(q == 3),
                           start=(q == 0), stop=(q == 3))
                pl.wait(ES, pl.mark("pe"))
                pl.emit(ES, lambda s, c=c, s_=s_, b2=bank: s.activation(
                    hT[:, c, 6 + s_ * 512: 6 + (s_ + 1) * 512], P[b2][:, 0:512],
                    ACT.Identity, scale=n1w[:, c:c + 1]), inc="ac")
                p_free[bank] = pl.mark("ac")
        if halo:
            bank = 2
            pl.wait(ET, p_free[bank])
            for c in range(NCH):
                transp(P[bank][:, c * 64:c * 64 + 6],
                       aux[0:6, TPB, c * 128:(c + 1) * 128], inc=(c == NCH - 1), np_=6,
                       start=(c == 0), stop=(c == NCH - 1))
            pl.wait(ES, pl.mark("pe"))
            for c in range(NCH):
                pl.emit(ES, lambda s, c=c, b2=bank: s.activation(
                    hT[:, c, 0:6], P[b2][:, c * 64:c * 64 + 6], ACT.Identity,
                    scale=n1w[:, c:c + 1]), inc="ac")
            p_free[bank] = pl.mark("ac")
            m_ht = pl.mark("ac")
        else:
            pl.waits(EV, [halo6_ready, pl.mark("ac")])
            m_ht = pl.emit(EV, lambda v: v.tensor_copy(
                hT[:, :, 0:6], halo6[:, :, :]), inc="dv")
        pl.waits(EV, [m_ht, pl.mark("ac")])
        halo6_ready = pl.emit(EV, lambda v: v.tensor_copy(
            halo6[:, :, :], hT[:, :, TB:TB + 6]), inc="dv")
        aux_scaled_free = pl.mark("pe")

        pl.waits(EG, [scr_mixer_gate, m_chain2])
        pl.emit(EG, lambda g: g.dma_start(
            out=wk_sl[:, :, :], in_=dr["Wk"].rearrange("(k p) m -> p k m", p=128)),
            inc="mxs", inck=16)
        pl.emit(EG, lambda g: g.dma_start(
            out=mt_sl[:, :, :], in_=dr["Mt"].rearrange("(k p) m -> p k m", p=128)),
            inc="mxs", inck=16)
        pl.emit(EG, lambda g: g.dma_start(
            out=wme_sl[:, :, :], in_=dr["Wmem"].rearrange("(k p) m -> p k m", p=128)),
            inc="mxs", inck=16)
        m_mixw = pl.mark("mxs")

        if in_state:
            m_state_done = None
            m_state_prev = None
            for half in range(2):
                pl.waits(EG, [scr_mixer_gate, m_chain2])
                if half == 1:
                    pl.waits(EG, [m_half_mm])
                pl.emit(EG, lambda g, h=half: g.dma_start(
                    out=wdg_h[:, 0, :, :, :],
                    in_=dr["Wdecay_t"][4 * h:4 * h + 4].rearrange(
                        "o k p m -> p o k m")), inc="mxs", inck=16)
                pl.emit(EG, lambda g, h=half: g.dma_start(
                    out=wdg_h[:, 1, :, :, :],
                    in_=dr["Wgate_t"][4 * h:4 * h + 4].rearrange(
                        "o k p m -> p o k m")), inc="mxs", inck=16)
                m_wdg = pl.mark("mxs")
                for ci in range(4):
                    c = half * 4 + ci
                    for s2 in range(NSL):
                        hs = lambda k: hT[:, k, 6 + s2 * 512: 6 + (s2 + 1) * 512]
                        bank = 0
                        pl.waits(ET, [m_wdg, m_ht, p_free[bank]])
                        for k in range(NCH):
                            mm(P[bank][:, 0:512], wdg_h[:, 0, ci, k, :], hs(k),
                               k == 0, k == NCH - 1, inc=(k == NCH - 1))
                        pl.waits(ES, [pl.mark("pe"), m_state_prev])
                        pl.emit(ES, lambda s, c=c, b2=bank: s.activation(
                            st_scr[0][:, :], P[b2][:, 0:512], ACT.Sigmoid,
                            bias=bdec[:, c:c + 1]), inc="ac")
                        m_dec = pl.mark("ac")
                        p_free[bank] = m_dec
                        bank = 1
                        pl.wait(ET, p_free[bank])
                        for k in range(NCH):
                            mm(P[bank][:, 0:512], wdg_h[:, 1, ci, k, :], hs(k),
                               k == 0, k == NCH - 1, inc=(k == NCH - 1))
                        pl.wait(ES, pl.mark("pe"))
                        pl.emit(ES, lambda s, c=c, b2=bank: s.activation(
                            st_scr[3][:, :], P[b2][:, 0:512], ACT.Sigmoid,
                            bias=bgat[:, c:c + 1]), inc="ac")
                        m_gat = pl.mark("ac")
                        p_free[bank] = m_gat
                        pl.wait(EV, m_dec)
                        m_clip = pl.emit(EV, lambda v: v.tensor_scalar(
                            st_scr[0][:, :], st_scr[0][:, :], 0.05, 0.95,
                            ALU.max, ALU.min), inc="dv", wide=_w('state'))
                        pl.wait(ES, m_clip)
                        pl.emit(ES, lambda s: s.activation(
                            st_scr[1][:, :], st_scr[0][:, :], ACT.Ln), inc="ac")
                        m_ldv = pl.mark("ac")
                        pl.wait(EV, m_ldv)
                        pl.emit(EV, lambda v, c=c: v.tensor_tensor_scan(
                            st_scr[2][:, :], st_scr[1][:, :], st_scr[1][:, :],
                            lcd_carry[:, c:c + 1], ALU.add, ALU.bypass), wide=_w('state'))
                        pl.emit(EV, lambda v, c=c: v.tensor_copy(
                            lcd_carry[:, c:c + 1], st_scr[2][:, 511:512]))
                        pl.emit(EV, lambda v: v.tensor_sub(
                            st_scr[1][:, :], st_scr[1][:, :], st_scr[2][:, :]), wide=_w('state'))
                        m_ze = pl.emit(EV, lambda v: v.tensor_scalar_min(
                            st_scr[1][:, :], st_scr[1][:, :], float(EXP_CLAMP)),
                            inc="dv", wide=_w('state'))
                        pl.wait(ES, m_ze)
                        pl.emit(ES, lambda s: s.activation(
                            st_scr[0][:, :], st_scr[1][:, :], ACT.Exp), inc="ac")
                        m_e1 = pl.mark("ac")
                        pl.waits(EV, [m_e1, m_gat])
                        pl.emit(EV, lambda v: v.tensor_scalar_min(
                            st_scr[0][:, :], st_scr[0][:, :], float(RECIP_MAX)), wide=_w('state'))
                        pl.emit(EV, lambda v, c=c, s2=s2: v.tensor_mul(
                            st_scr[3][:, :], st_scr[3][:, :],
                            hT[:, c, 6 + s2 * 512: 6 + (s2 + 1) * 512]), wide=_w('state'))
                        pl.emit(EV, lambda v: v.tensor_mul(
                            st_scr[3][:, :], st_scr[3][:, :], st_scr[0][:, :]), wide=_w('state'))
                        pl.emit(EV, lambda v, c=c: v.tensor_tensor_scan(
                            st_scr[1][:, :], st_scr[3][:, :], st_scr[3][:, :],
                            cs2_carry[:, c:c + 1], ALU.add, ALU.bypass), wide=_w('state'))
                        m_cs2 = pl.emit(EV, lambda v, c=c: v.tensor_copy(
                            cs2_carry[:, c:c + 1], st_scr[1][:, 511:512]), inc="dv")
                        pl.wait(ES, m_cs2)
                        pl.emit(ES, lambda s: s.activation(
                            st_scr[0][:, :], st_scr[2][:, :], ACT.Exp), inc="ac")
                        m_e2 = pl.mark("ac")
                        pl.wait(EV, m_e2)
                        m_state_done = pl.emit(
                            EV, lambda v, c=c, s2=s2: v.scalar_tensor_tensor(
                                h2T[:, c, s2 * 512:(s2 + 1) * 512], st_scr[1][:, :],
                                ssm_col[:, 0:1], st_scr[0][:, :],
                                ALU.mult, ALU.mult), inc="dv", wide=_w('state'))
                        m_state_prev = m_state_done
                m_half_mm = pl.mark("pe")
        else:
            m_state_done = None

        for s2 in range(NSL):
            bank = 0
            for o2 in range(2):
                pl.waits(ET, [m_mixw, m_ht, p_free[bank]])
                for k in range(NCH):
                    mm(P[bank][:, 0:512], wk_sl[:, k, o2 * 128:(o2 + 1) * 128],
                       hT[:, k, 6 + s2 * 512: 6 + (s2 + 1) * 512],
                       k == 0, k == NCH - 1, inc=(k == NCH - 1))
                pl.wait(ES, pl.mark("pe"))
                pl.emit(ES, lambda s, o2=o2, b2=bank: s.copy(
                    kT_s[:, o2, :], P[b2][:, 0:512]), inc="ac")
                p_free[bank] = pl.mark("ac")
            m_kT = pl.mark("ac")
            bank = 1
            for o2 in range(2):
                pl.waits(ET, [m_kT, p_free[bank]])
                for k in range(2):
                    mm(P[bank][:, 0:512], mt_sl[:, k, o2 * 128:(o2 + 1) * 128],
                       kT_s[:, k, :], k == 0, k == 1, inc=(k == 1))
                pl.wait(ES, pl.mark("pe"))
                pl.emit(ES, lambda s, o2=o2, b2=bank: s.copy(
                    moT_s[:, o2, :], P[b2][:, 0:512]), inc="ac")
                p_free[bank] = pl.mark("ac")
            m_moT = pl.mark("ac")
            for o in range(NCH):
                bank = o % 2
                pl.waits(ET, [m_moT, p_free[bank]])
                for k in range(2):
                    mm(P[bank][:, 0:512], wme_sl[:, k, o * 128:(o + 1) * 128],
                       moT_s[:, k, :], k == 0, k == 1, inc=(k == 1))
                pl.wait(ES, pl.mark("pe"))
                pl.emit(ES, lambda s, o=o, s2=s2, b2=bank: s.activation(
                    memT_b[:, o, s2 * 512:(s2 + 1) * 512], P[b2][:, 0:512],
                    ACT.Identity, scale=ms_col[:, 0:1]), inc="ac")
                p_free[bank] = pl.mark("ac")
        m_memT = pl.mark("ac")

        m_local = None
        if in_state:
            cv_slots = [sqs[:, 0:512], sqs[:, 512:1024]]
            depth = 1
        else:
            h2f = h2T[:, :, :].rearrange("p a b -> p (a b)").bitcast(F32)
            cv_slots = ([h2f[:, i * 512:(i + 1) * 512] for i in range(8)]
                        + [sqs[:, 0:512], sqs[:, 512:1024]])
            depth = 5
        nslot = len(cv_slots)
        first_copy = True

        def cv_copy(cc):
            nonlocal m_local, first_copy
            if first_copy:
                pl.waits(EV, [m_memT, m_state_done])
                first_copy = False
            for s2 in range(NSL):
                d = cv_slots[(cc * NSL + s2) % nslot]
                m_local = pl.emit(EV, lambda v, cc=cc, s2=s2, d=d:
                                  v.tensor_copy(
                                      hT[:, cc, 6 + s2 * 512: 6 + (s2 + 1) * 512],
                                      d[:, :]), inc="dv", wide=_w('conv'))

        for c in range(NCH):
            if c >= depth:
                cv_copy(c - depth)
            pl.wait(EV, m_ht)
            for s2 in range(NSL):
                d = cv_slots[(c * NSL + s2) % nslot]
                pl.emit(EV, lambda v, c=c, s2=s2, d=d: v.tensor_scalar_mul(
                    d[:, :], hT[:, c, s2 * 512 + 0: s2 * 512 + 512],
                    convw[:, c, 0:1]), wide=_w('conv'))
                for j in range(1, 7):
                    pl.emit(EV, lambda v, c=c, s2=s2, j=j, d=d:
                            v.scalar_tensor_tensor(
                                d[:, :], hT[:, c, s2 * 512 + j: s2 * 512 + 512 + j],
                                convw[:, c, j:j + 1], d[:, :], ALU.mult, ALU.add),
                            wide=_w('conv'))
        for cc in range(NCH - depth, NCH):
            cv_copy(cc)

        m_mix_done_parts = []
        m_rebuild = None
        for o in range(NCH):
            slot = o % 2
            sem = wc_sem[slot]
            pl.waits(EG, [wc_slot_free[slot], scr_mixer_gate, m_chain2, m_memT])
            if in_state:
                pl.emit(EG, lambda g, o=o, sl=slot: g.dma_start(
                    out=wc_ring[sl][:, :, :],
                    in_=dr["Wc_t"][o].rearrange("k p m -> p k m")),
                    inc=sem, inck=16)
                wc_slot_count[slot] += 1
            else:
                pl.emit(EG, lambda g, o=o, sl=slot: g.dma_start(
                    out=wc_ring[sl][:, 0:NCH, :],
                    in_=dr["Wc_t"][o, 0:NCH].rearrange("k p m -> p k m")),
                    inc=sem, inck=16)
                pl.emit(EG, lambda g, o=o, sl=slot: g.dma_start(
                    out=wc_ring[sl][:, 2 * NCH:3 * NCH, :],
                    in_=dr["Wc_t"][o, 2 * NCH:3 * NCH].rearrange("k p m -> p k m")),
                    inc=sem, inck=16)
                wc_slot_count[slot] += 2
            m_wcl = (sem, wc_slot_count[slot] * 16)
            wc_parts = {"mixed_L": "L", "mixed_S": "S", "mixed_M": "M"}.get(
                stop_after, "LSM")
            for s2 in range(NSL):
                bank = s2
                pl.waits(ET, [m_wcl, p_free[bank]])
                plan_mm = []
                if "M" in wc_parts:
                    for k in range(NCH):
                        plan_mm.append((wc_ring[slot][:, 2 * NCH + k, :],
                                        memT_b[:, k, s2 * 512:(s2 + 1) * 512], None))
                if "S" in wc_parts and in_state:
                    for k in range(NCH):
                        plan_mm.append((wc_ring[slot][:, NCH + k, :],
                                        h2T[:, k, s2 * 512:(s2 + 1) * 512],
                                        m_state_done if k == 0 else None))
                if "L" in wc_parts:
                    for k in range(NCH):
                        plan_mm.append((wc_ring[slot][:, k, :],
                                        hT[:, k, 6 + s2 * 512: 6 + (s2 + 1) * 512],
                                        m_local if k == 0 else None))
                for i, (lh, rh, mw) in enumerate(plan_mm):
                    if mw is not None:
                        pl.wait(ET, mw)
                    mm(P[bank][:, 0:512], lh, rh, i == 0,
                       i == len(plan_mm) - 1, inc=(i == len(plan_mm) - 1))
                pl.wait(ES, pl.mark("pe"))
                pl.emit(ES, lambda s, s2=s2, o=o, b2=bank: s.activation(
                    combT[s2][:, :], P[b2][:, 0:512], ACT.Identity,
                    bias=bc_c[:, o:o + 1]), inc="ac")
                m_cmb = pl.mark("ac")
                p_free[bank] = m_cmb
                bank2 = 2 + (s2 % 2)
                pl.waits(ET, [m_cmb, p_free[bank2], aux_scaled_free])
                for q in range(4):
                    transp(P[bank2][:, q * 128:(q + 1) * 128],
                           combT[s2][:, q * 128:(q + 1) * 128],
                           inc=(q == 3), start=(q == 0), stop=(q == 3))
                m_trc = pl.mark("pe")
                if o == 0:
                    pl.wait(ES, m_trc)
                    for q in range(4):
                        t = s2 * 4 + q
                        pl.emit(ES, lambda s, t=t, c=col0 + t: s.activation(
                            aux[:, t, :], xbuf[:, t, :], ACT.Identity,
                            scale=iv2[:, c:c + 1], bias=nm2[:, c:c + 1]), inc="ac")
                    m_rebuild = pl.mark("ac")
                pl.waits(EV, [m_trc, m_rebuild])
                m_add = pl.emit(EV, lambda v, s2=s2, o=o, b2=bank2: v.tensor_add(
                    aux[:, s2 * 4:(s2 + 1) * 4, o * 128:(o + 1) * 128],
                    aux[:, s2 * 4:(s2 + 1) * 4, o * 128:(o + 1) * 128],
                    P[b2][:, 0:512].rearrange("p (q w) -> p q w", w=128)), inc="dv",
                    wide=_w('misc'))
                p_free[bank2] = m_add
            wc_slot_free[slot] = pl.mark("pe")
        m_mixed = pl.mark("dv")
        m_mixer_mm_done = pl.mark("pe")

        if stop_after in ("mixed", "mixed_L", "mixed_S", "mixed_M"):
            pl.waits(EY, [m_mixed])
            m_yst = pl.emit(EY, lambda y, b=b: y.dma_start(
                out=y_d[b * TB:(b + 1) * TB, :].rearrange("(k p) d -> p k d", p=128),
                in_=aux[:, 0:TPB, :]), inc="xst", inck=16)
            xbuf_free = m_yst
            aux_free = m_yst
            scr_mixer_gate = m_mixer_mm_done
            continue

        pl.wait(EV, m_mixed)
        pl.wait(ES, m_mixed)
        for t in range(TPB):
            stats_tile(aux[:, t, :], col0 + t, mu1)
        m_r3 = stat_cols(mu1, None, None, col0, col0 + TPB, 0.0, r_t=rr3,
                         r_only=True)
        m_x2a = pl.emit(EV, lambda v: v.tensor_add(
            xbuf[:, 0:TPB, :], xbuf[:, 0:TPB, :], aux[:, 0:TPB, :]), inc="dv",
            wide=_w('scale'))

        pl.waits(EV, [m_r3, m_x2a])
        for t in range(TPB):
            pl.emit(EV, lambda v, t=t, c=col0 + t: v.tensor_scalar_mul(
                aux[:, t, :], aux[:, t, :], rr3[:, c:c + 1]), inc="dv", wide=_w('scale'))
        m_scl = pl.mark("dv")
        pl.waits(ET, [m_scl])
        for s_ in range(NSL):
            for c in range(NCH):
                bank = (s_ * NCH + c) % 2
                pl.wait(ET, p_free[bank])
                for q in range(4):
                    transp(P[bank][:, q * 128:(q + 1) * 128],
                           aux[:, s_ * 4 + q, c * 128:(c + 1) * 128], inc=(q == 3),
                           start=(q == 0), stop=(q == 3))
                pl.wait(ES, pl.mark("pe"))
                pl.emit(ES, lambda s, c=c, s_=s_, b2=bank: s.activation(
                    h2T[:, c, s_ * 512:(s_ + 1) * 512], P[b2][:, 0:512],
                    ACT.Identity, scale=n2w[:, c:c + 1]), inc="ac")
                p_free[bank] = pl.mark("ac")
        m_h2T = pl.mark("ac")
        aux_ff_free = pl.mark("pe")

        ff_mark = {}
        pf_free4 = [None] * 4
        m_prev_pr = [None, None]
        m_prev_wdown = None
        import os as _os
        _NG = int(_os.environ.get('SWG_GROUPS', '4'))
        for g_ in range(_NG):
            for st in range(NST):
                for hh in range(NCH):
                    h = g_ * NCH + hh
                    slot = h % NRING
                    sem = swg_sem[slot]
                    if st == 0:
                        pl.waits(EG, [swg_slot_free[slot], m_mixer_mm_done])
                        pl.emit(EG, lambda g, h=h, sl=slot: g.dma_start(
                            out=swg_wg[sl][:, :, :],
                            in_=dr["Wg_t"][h].rearrange("(k p) m -> p k m", p=128)),
                            inc=sem, inck=16)
                        pl.emit(EG, lambda g, h=h, sl=slot: g.dma_start(
                            out=swg_wu[sl][:, :, :],
                            in_=dr["Wup_t"][h].rearrange("(k p) m -> p k m", p=128)),
                            inc=sem, inck=16)
                        pl.emit(EG, lambda g, h=h, sl=slot: g.dma_start(
                            out=swg_wd[sl][:, :, :],
                            in_=dr["Wdown"][h * 128:(h + 1) * 128, :]
                            .rearrange("p (a b) -> p a b", b=128)),
                            inc=sem, inck=16)
                        swg_slot_count[slot] += 3
                    m_wl = (sem, swg_slot_count[slot] * 16)
                    bg = 2 * (hh % 2)
                    bu = bg + 1
                    pl.waits(ET, [m_wl, m_h2T, p_free[bg], p_free[bu]])
                    for k in range(NCH):
                        mm(P[bg][:, 0:256], swg_wg[slot][:, k, :],
                           h2T[:, k, st * 256:(st + 1) * 256],
                           k == 0, k == NCH - 1, inc=False)
                    for k in range(NCH):
                        mm(P[bu][:, 0:256], swg_wu[slot][:, k, :],
                           h2T[:, k, st * 256:(st + 1) * 256],
                           k == 0, k == NCH - 1, inc=(k == NCH - 1))
                    m_gu = pl.mark("pe")
                    pl.waits(ES, [m_gu, m_prev_pr[hh % 2]])
                    pl.emit(ES, lambda s, b2=bg, pb=hh % 2: s.activation(
                        silu_sb[pb][:, :], P[b2][:, 0:256], ACT.Silu), inc="ac")
                    m_si = pl.mark("ac")
                    pl.waits(EV, [m_si, m_prev_wdown])
                    m_pr = pl.emit(EV, lambda v, b2=bu, hh=hh, pb=hh % 2:
                                   v.tensor_mul(prod_buf[:, hh, :],
                                                silu_sb[pb][:, :],
                                                P[b2][:, 0:256]), inc="dv",
                                   wide=_w('misc'))
                    m_prev_pr[hh % 2] = m_pr
                    p_free[bg] = m_pr
                    p_free[bu] = m_pr
                m_prods = pl.mark("dv")
                for o in range(NCH):
                    pfb = o % 4
                    region = PF[:, pfb * 512: pfb * 512 + 256]
                    pl.waits(ET, [m_prods, pf_free4[pfb]])
                    for hh in range(NCH):
                        slot = (g_ * NCH + hh) % NRING
                        mm(region, swg_wd[slot][:, o, :], prod_buf[:, hh, :],
                           hh == 0, hh == NCH - 1, inc=(hh == NCH - 1))
                    m_pf = pl.mark("pe")
                    key = (st, o)
                    if g_ == 0:
                        pl.waits(ES, [m_pf, aux_ff_free])
                        def fcp(s, st=st, o=o, r=region):
                            return s.copy(aux[:, o, st * 256:(st + 1) * 256], r)
                        ff_mark[key] = pl.emit(ES, fcp, inc="ac")
                    else:
                        pl.waits(EV, [m_pf, ff_mark[key]])
                        def fad(v, st=st, o=o, r=region):
                            return v.tensor_add(
                                aux[:, o, st * 256:(st + 1) * 256],
                                aux[:, o, st * 256:(st + 1) * 256], r)
                        ff_mark[key] = pl.emit(EV, fad, inc="dv", wide=_w('misc'))
                    pf_free4[pfb] = ff_mark[key]
                m_prev_wdown = pl.mark("pe")
            for hh in range(NCH):
                slot = (g_ * NCH + hh) % NRING
                swg_slot_free[slot] = pl.mark("pe")
        m_ff = pl.mark("dv")
        m_swg_done = pl.mark("pe")

        if stop_after == "swg":
            pl.waits(EY, [m_ff, pl.mark("ac"), m_swg_done])
            m_yst = pl.emit(EY, lambda y, b=b: y.dma_start(
                out=y_d[b * TB:(b + 1) * TB, :].rearrange("(k p) d -> p k d", p=128),
                in_=xbuf[:, 0:TPB, :]), inc="xst", inck=16)
            xbuf_free = m_yst
            aux_free = m_yst
            scr_mixer_gate = m_swg_done
            continue

        def ff_back_slice(s2):
            best = {}
            for st in (2 * s2, 2 * s2 + 1):
                for o in range(NCH):
                    sem, cnt = ff_mark[(st, o)]
                    best[sem] = max(best.get(sem, 0), cnt)
            gates = list(best.items())
            m_addf = None
            for o in range(NCH):
                bank = 2 + ((s2 * NCH + o) % 2)
                pl.waits(ET, gates + [p_free[bank]])
                for q in range(4):
                    transp(P[bank][:, q * 128:(q + 1) * 128],
                           aux[:, o, s2 * 512 + q * 128: s2 * 512 + (q + 1) * 128],
                           inc=(q == 3), start=(q == 0), stop=(q == 3))
                m_trf = pl.mark("pe")
                pl.wait(EV, m_trf)
                m_addf = pl.emit(EV, lambda v, s2=s2, o=o, b2=bank: v.tensor_add(
                    xbuf[:, s2 * 4:(s2 + 1) * 4, o * 128:(o + 1) * 128],
                    xbuf[:, s2 * 4:(s2 + 1) * 4, o * 128:(o + 1) * 128],
                    P[b2][:, 0:512].rearrange("p (q w) -> p q w", w=128)), inc="dv")
                p_free[bank] = m_addf
            return m_addf

        if stop_after == "x2":
            for s2 in range(NSL):
                ff_back_slice(s2)
            m_x2 = pl.mark("dv")
            pl.wait(EY, m_x2)
            m_yst = pl.emit(EY, lambda y, b=b: y.dma_start(
                out=y_d[b * TB:(b + 1) * TB, :].rearrange("(k p) d -> p k d", p=128),
                in_=xbuf[:, 0:TPB, :]), inc="xst", inck=16)
            xbuf_free = m_yst
            aux_free = pl.mark("pe")
            scr_mixer_gate = m_swg_done
            continue

        m_adds = [ff_back_slice(s2) for s2 in range(NSL)]
        aux_free = pl.mark("pe")

        m_st25 = []
        m_park = []
        for s2 in range(NSL):
            pl.wait(EY, m_adds[s2])
            m_park.append(pl.emit(EY, lambda y, b=b, s2=s2: y.dma_start(
                out=y_d[b * TB + s2 * 512:b * TB + (s2 + 1) * 512, :]
                .rearrange("(k p) d -> p k d", p=128),
                in_=xbuf[:, s2 * 4:(s2 + 1) * 4, :]), inc="xst", inck=16))
            pl.wait(ES, m_adds[s2])
            pl.wait(EV, m_adds[s2])
            for q in range(4):
                stats_tile(xbuf[:, s2 * 4 + q, :], col0 + s2 * 4 + q, mu1)
            m_st25.append(stat_cols(mu1, iv1, nm1, col0 + s2 * 4,
                                    col0 + s2 * 4 + 4, LN_EPS))
        m_x2st = m_park[-1]
        for s2 in range(NSL):
            c0s = col0 + s2 * 4
            pl.waits(ES, [m_st25[s2], m_park[s2]])
            for q in range(4):
                pl.emit(ES, lambda s, t=s2 * 4 + q, c=c0s + q: s.activation(
                    xbuf[:, t, :], xbuf[:, t, :], ACT.Identity,
                    scale=iv1[:, c:c + 1], bias=nm1[:, c:c + 1]), inc="ac")
            m_lnx2 = pl.mark("ac")
            pl.waits(ET, [m_lnx2, m_swg_done])
            for c in range(NCH):
                bank = c % 2
                pl.wait(ET, p_free[bank])
                for q in range(4):
                    transp(P[bank][:, q * 128:(q + 1) * 128],
                           xbuf[:, s2 * 4 + q, c * 128:(c + 1) * 128], inc=(q == 3),
                           start=(q == 0), stop=(q == 3))
                pl.waits(ES, [pl.mark("pe"), accb_free])
                pl.emit(ES, lambda s, c=c, s2=s2, b2=bank: s.activation(
                    outT[:, c, 0:512], P[b2][:, 0:512], ACT.Identity,
                    accum_out=accb[:, c, s2:s2 + 1]), inc="ac")
                p_free[bank] = pl.mark("ac")
            m_tr25 = pl.mark("ac")
            pl.wait(ET, m_tr25)
            mz = chain_slice(lambda k: outT[:, k, 0:512])
            zrow_to_cols(z1pc, c0s, 4, mz)
        x2st_marks.append(m_x2st)
        m_evac25 = pl.mark("ac")
        pl.wait(EV, m_evac25)
        pl.emit(EV, lambda v: v.tensor_reduce(
            accb[:, :, 0:1], accb[:, :, :], AX.X, ALU.add))
        m_sums2 = pl.emit(EV, lambda v: v.tensor_add(
            sums[1][:, :], sums[1][:, :], accb[:, :, 0]), inc="dv")
        accb_free = m_sums2
        scr_mixer_gate = pl.mark("pe")
        xbuf_free = pl.mark("pe")

    if stop_after is not None:
        pl.wait(EY, pl.mark("xst"))
        with nc.Block() as block:
            @block.sync
            def _(eng):
                for f in pl.q["sync"]:
                    f(eng)

            @block.tensor
            def _(eng):
                for f in pl.q["tensor"]:
                    f(eng)

            @block.vector
            def _(eng):
                for f in pl.q["vector"]:
                    f(eng)

            @block.scalar
            def _(eng):
                for f in pl.q["scalar"]:
                    f(eng)

            @block.gpsimd
            def _(eng):
                for f in pl.q["gpsimd"]:
                    f(eng)
        es.close()
        return nc

    bufs3 = [xbuf, aux]
    buf3_free = [xbuf_free, aux_free]

    def p3_load_apply(b):
        B3 = bufs3[b % 2]
        pl.waits(EY, [buf3_free[b % 2], x2st_marks[b]])
        pl.emit(EY, lambda y, b=b, B3=B3: y.dma_start(
            out=B3[:, 0:TPB, :],
            in_=y_d[b * TB:(b + 1) * TB, :].rearrange("(k p) d -> p k d", p=128)),
            inc="xld", inck=16)
        m_xld = pl.mark("xld")
        pl.wait(ES, m_xld)
        for t in range(TPB):
            pl.emit(ES, lambda s, t=t, c=b * TPB + t, B3=B3: s.activation(
                B3[:, t, :], B3[:, t, :], ACT.Identity,
                scale=iv1[:, c:c + 1], bias=nm1[:, c:c + 1]), inc="ac")
        return pl.mark("ac")

    pre3 = {b: p3_load_apply(b) for b in range(min(2, NBLK))}

    m_gfB = allreduce_gf(1, m_sums2)

    for b in range(NBLK):
        col0 = b * TPB
        B3 = bufs3[b % 2]
        m_apply = pre3[b] if b in pre3 else p3_load_apply(b)
        m_chain3, m_aux_upd, _, _ = ffn_cycles(z1pc, 1, b, False, m_apply,
                                               buf=B3)
        pl.wait(EV, m_aux_upd)
        m_y = pl.emit(EV, lambda v, B3=B3: v.tensor_scalar_mul(
            B3[:, 0:TPB, :], B3[:, 0:TPB, :], 0.5), inc="dv", wide=_w('scale'))
        pl.waits(EG, [m_y, x2st_marks[b]])
        m_yac = pl.emit(EG, lambda g, b=b, B3=B3: g.dma_start(
            out=y_d[b * TB:(b + 1) * TB, :].rearrange("(k p) d -> p k d", p=128),
            in_=B3[:, 0:TPB, :], accum_op=ALU.add), inc="yac", inck=16)
        buf3_free[b % 2] = m_yac

    pl.wait(EY, pl.mark("xst"))
    pl.wait(EY, pl.mark("yac"))

    with nc.Block() as block:
        @block.sync
        def _(eng):
            for f in pl.q["sync"]:
                f(eng)

        @block.tensor
        def _(eng):
            for f in pl.q["tensor"]:
                f(eng)

        @block.vector
        def _(eng):
            for f in pl.q["vector"]:
                f(eng)

        @block.scalar
        def _(eng):
            for f in pl.q["scalar"]:
                f(eng)

        @block.gpsimd
        def _(eng):
            for f in pl.q["gpsimd"]:
                f(eng)

    es.close()
    return nc


_BUILD_CACHE = {}


def _get_program(key):
    if key not in _BUILD_CACHE:
        TC, TB, stt, seq = key
        _BUILD_CACHE[key] = build_program(TC, TB, stt, seq)
    return _BUILD_CACHE[key]


def _prep_weights(inputs):
    F = np.float32

    def g(k):
        return np.ascontiguousarray(np.asarray(inputs[k], F))

    w = {}
    w["conv_w"] = g("L0_attn_conv_w")
    w["norm1_w"] = g("L0_attn_norm1_w")
    w["norm2_w"] = g("L0_attn_norm2_w")
    w["bdecay"] = g("L0_attn_bdecay")
    w["bgate"] = g("L0_attn_bgate")
    w["bc"] = g("L0_attn_bc")
    for i in range(1, 5):
        w[f"cn_w{i}"] = g(f"L0_ffn_cn_w{i}")
        w[f"cn_b{i}"] = g(f"L0_ffn_cn_b{i}")
    for i in range(1, 4):
        w[f"gc_w{i}"] = g(f"L0_ffn_gc_w{i}")
        w[f"gc_b{i}"] = g(f"L0_ffn_gc_b{i}")
    w["Wk"] = g("L0_attn_Wk")
    w["Mt"] = np.ascontiguousarray(g("L0_attn_M").T)
    w["Wmem"] = g("L0_attn_Wmem")
    w["Wdecay_t"] = np.ascontiguousarray(
        g("L0_attn_Wdecay").reshape(NCH, 128, NCH, 128).transpose(2, 0, 1, 3))
    w["Wgate_t"] = np.ascontiguousarray(
        g("L0_attn_Wgate").reshape(NCH, 128, NCH, 128).transpose(2, 0, 1, 3))
    w["Wc_t"] = np.ascontiguousarray(
        g("L0_attn_Wc").reshape(3 * NCH, 128, NCH, 128).transpose(2, 0, 1, 3))
    w["Wg_t"] = np.ascontiguousarray(
        g("L0_attn_Wg").reshape(D, 32, 128).transpose(1, 0, 2))
    w["Wup_t"] = np.ascontiguousarray(
        g("L0_attn_Wup").reshape(D, 32, 128).transpose(1, 0, 2))
    w["Wdown"] = g("L0_attn_Wdown")
    w["state_scale"] = np.asarray(inputs["L0_attn_state_scale"], F).reshape(1)
    w["mem_scale"] = np.asarray(inputs["L0_attn_mem_scale"], F).reshape(1)
    w["ar"] = np.asarray(inputs["L0_ffn_adaptive_range"], F).reshape(1)
    return w


def make_in_maps(inputs, TC):
    x = np.ascontiguousarray(np.asarray(inputs["L0_attn_x"], np.float32))
    w = _prep_weights(inputs)
    nchunk = x.shape[1] // TC
    in_maps = []
    for c in range(NCORE):
        b, k = c // nchunk, c % nchunk
        m = dict(w)
        m["x"] = np.ascontiguousarray(x[b, k * TC:(k + 1) * TC])
        m["xhalo"] = (np.ascontiguousarray(x[b, k * TC - 6:k * TC]) if k > 0
                      else np.zeros((6, D), np.float32))
        m["bm"] = np.ascontiguousarray(np.eye(2, dtype=np.float32)[b])
        m["smask"] = np.asarray([1.0 if k == 0 else 0.0], np.float32)
        in_maps.append(m)
    return in_maps


def kernel(**inputs):
    x = np.asarray(inputs["L0_attn_x"])
    B, T, _ = x.shape
    TC = T // 4
    nc = _get_program((TC, 1024, 2048, T))
    in_maps = make_in_maps(inputs, TC)
    res = run_bass_kernel_spmd(nc, in_maps, core_ids=list(range(NCORE)))
    y = np.zeros((B, T, D), np.float32)
    for c in range(NCORE):
        b, k = c // 4, c % 4
        y[b, k * TC:(k + 1) * TC] = res.results[c]["y"]
    return y
